# version 01 (final)
# baseline (speedup 1.0000x reference)
import contextlib

import numpy as np
import ml_dtypes

import concourse.bass as bass
import concourse.mybir as mybir
from concourse.bass_utils import run_bass_kernel_spmd

F32 = mybir.dt.float32
BF16 = mybir.dt.bfloat16
AF = mybir.ActivationFunctionType
ALU = mybir.AluOpType
AX = mybir.AxisListType.X

NCORES = 8
T = 1024
NJ = 8
D = 2048
DC = 16
O = 8192
OC = 64
HEADS = 16
HD = 128
SEQ = 8192
EPS1 = 1e-5
EPS2 = 1e-6
TEPS = 1e-6
HALF = 512
NJH = 4
RSQ_HD = 0.08838834764831843


class Prog:
    ENGINES = ("sync", "gpsimd", "vector", "scalar", "tensor")

    def __init__(self, nc, ctx):
        self.nc = nc
        self.ctx = ctx
        self.ops = {e: [] for e in self.ENGINES}
        self.sems = {}
        self.counts = {}
        self.seen = {e: {} for e in self.ENGINES}
        self.bank = {i: {"w": None, "r": []} for i in range(8)}

    def sem(self, name):
        if name not in self.sems:
            self.sems[name] = self.ctx.enter_context(self.nc.semaphore(name))
            self.counts[name] = 0
        return self.sems[name]

    def emit(self, eng, fn, wait=(), inc=None, dma=False):
        eff = []
        for tk in wait:
            if tk is None:
                continue
            s, v = tk
            if self.seen[eng].get(s, -1) >= v:
                continue
            eff.append((s, v))
            self.sem(s)
            self.seen[eng][s] = max(self.seen[eng].get(s, -1), v)
        tok = None
        amt = 16 if dma else 1
        if inc is not None:
            self.sem(inc)
            self.counts[inc] += amt
            tok = (inc, self.counts[inc])
            if not dma:
                self.seen[eng][inc] = self.counts[inc]
        self.ops[eng].append((eff, fn, inc, amt))
        return tok

    def pe_write(self, banks, fn, wait=(), inc="pe"):
        w = list(wait)
        for b in banks:
            w += self.bank[b]["r"]
        tok = self.emit("tensor", fn, wait=w, inc=inc)
        for b in banks:
            self.bank[b]["w"] = tok
            self.bank[b]["r"] = []
        return tok

    def pe_acc(self, banks, fn, wait=(), inc=None):
        tok = self.emit("tensor", fn, wait=wait, inc=inc)
        if inc is not None:
            for b in banks:
                self.bank[b]["w"] = tok
        return tok

    def bank_read(self, eng, banks, fn, wait=(), inc=None):
        w = list(wait) + [self.bank[b]["w"] for b in banks]
        tok = self.emit(eng, fn, wait=w, inc=inc if inc else
                        ("ve" if eng == "vector" else "sc"))
        for b in banks:
            self.bank[b]["r"].append(tok)
        return tok

    def build(self):
        sems = self.sems
        with self.nc.Block() as block:
            for e in self.ENGINES:
                if not self.ops[e]:
                    continue
                ops = self.ops[e]

                def mk(ops):
                    def body(eng):
                        for waits, fn, inc, amt in ops:
                            for (s, v) in waits:
                                eng.wait_ge(sems[s], v)
                            inst = fn(eng)
                            if inc is not None:
                                inst.then_inc(sems[inc], amt)
                    return body

                getattr(block, e)(mk(ops))


class Rot:

    def __init__(self, views):
        self.views = views
        self.last = [[] for _ in views]

    def get(self, i):
        k = i % len(self.views)
        return self.views[k], list(self.last[k])

    def used(self, i, *toks):
        self.last[i % len(self.views)] = [t for t in toks if t is not None]

    def used_add(self, i, *toks):
        self.last[i % len(self.views)] += [t for t in toks if t is not None]

    def seed(self, *toks):
        for k in range(len(self.views)):
            self.last[k] = [t for t in toks if t is not None]


class Cols:
    def __init__(self, t, n):
        self.t, self.n, self.pos, self.slots = t, n, 0, {}

    def __call__(self, name, w=1):
        if name not in self.slots:
            assert self.pos + w <= self.n, f"col overflow {name}"
            self.slots[name] = (self.pos, w)
            self.pos += w
        off, ww = self.slots[name]
        assert ww == w, name
        return self.t[:, off:off + w]


def build_launch1(parts=("s1", "qkv")):
    nc = bass.Bass("TRN2", target_bir_lowering=False, debug=False,
                   detect_race_conditions=False)

    x_in = nc.dram_tensor("x_in", [T, D], F32, kind="ExternalInput").ap()
    w_fc = nc.dram_tensor("w_fc", [O, D], F32, kind="ExternalInput").ap()
    w_proj = nc.dram_tensor("w_proj", [D, O], F32, kind="ExternalInput").ap()
    w_qkv = nc.dram_tensor("w_qkv", [D, 3 * D], F32, kind="ExternalInput").ap()
    ident_in = nc.dram_tensor("ident", [128, 128], BF16, kind="ExternalInput").ap()

    x1_out = nc.dram_tensor("x1", [T, D], F32, kind="ExternalOutput").ap()
    q_out = nc.dram_tensor("q_fm", [D, T], BF16, kind="ExternalOutput").ap()
    k_out = nc.dram_tensor("k_fm", [D, T], BF16, kind="ExternalOutput").ap()
    v_out = nc.dram_tensor("v_tm", [T, D], BF16, kind="ExternalOutput").ap()
    tfc_out = nc.dram_tensor("ternT_fc", [D, O], BF16, kind="ExternalOutput").ap()
    tpr_out = nc.dram_tensor("ternT_proj", [O, D], BF16, kind="ExternalOutput").ap()
    gam_out = nc.dram_tensor("gam", [1, 8], F32, kind="ExternalOutput").ap()
    if "dbg" in parts:
        dbg_xn = nc.dram_tensor("dbg_xn", [128, DC * T], BF16, kind="ExternalOutput").ap()
        dbg_h = nc.dram_tensor("dbg_h", [128, OC * HALF], BF16, kind="ExternalOutput").ap()
        dbg_st = nc.dram_tensor("dbg_st", [128, 256], F32, kind="ExternalOutput").ap()
        dbg_rw = nc.dram_tensor("dbg_rw", [1, 5 * HALF + 128 + D], F32, kind="ExternalOutput").ap()
        dbg_hA = nc.dram_tensor("dbg_hA", [128, OC * HALF], BF16, kind="ExternalOutput").ap()
        dbg_rwA = nc.dram_tensor("dbg_rwA", [1, 5 * HALF + 128 + D], F32, kind="ExternalOutput").ap()
    s2_out = nc.dram_tensor("s2row", [1, D], F32, kind="ExternalOutput").ap()

    ctx = contextlib.ExitStack()
    with ctx:
        E = ctx.enter_context
        x_tm = E(nc.sbuf_tensor("x_tm", [128, NJ * D], F32)).ap()
        xn_fm = E(nc.sbuf_tensor("xn_fm", [128, DC * T], BF16)).ap()
        h_fm = E(nc.sbuf_tensor("h_fm", [128, OC * HALF], BF16)).ap()
        xn_tm = h_fm[:, D:3 * D]
        WT = 1024
        wbuf = E(nc.sbuf_tensor("wbuf", [128, 2 * WT], F32)).ap()
        sgn = E(nc.sbuf_tensor("sgn", [128, WT], BF16)).ap()
        tnat = E(nc.sbuf_tensor("tnat", [128, 2 * WT], BF16)).ap()
        tT = E(nc.sbuf_tensor("tT", [128, 2 * DC * 128], BF16)).ap()
        w2b = E(nc.sbuf_tensor("w2b", [128, 2 * 512], BF16)).ap()
        sqf = h_fm[:, 0:D]
        hsq = E(nc.sbuf_tensor("hsq", [128, 2 * HALF], BF16)).ap()
        stats = E(nc.sbuf_tensor("stats", [128, 256], F32)).ap()
        rows = E(nc.sbuf_tensor("rows", [1, 5 * HALF + 128 + D], F32)).ap()
        ident = E(nc.sbuf_tensor("identsb", [128, 128], BF16)).ap()
        identf = E(nc.sbuf_tensor("identfsb", [128, 128], F32)).ap()
        onescol = E(nc.sbuf_tensor("onescol", [128, 1], BF16)).ap()
        onescf = E(nc.sbuf_tensor("onescf", [128, 1], F32)).ap()
        gval = E(nc.sbuf_tensor("gval", [1, 8], F32)).ap()
        gacc = E(nc.sbuf_tensor("gacc", [128, 128], F32)).ap()
        gcol = E(nc.sbuf_tensor("gcol", [128, 6], F32)).ap()
        econ = E(nc.sbuf_tensor("econ", [128, 3], F32)).ap()
        qkvw = wbuf.bitcast(BF16)
        qkst = tT
        vst = tnat

        pbank = [E(nc.psum_tensor(f"pb{i}", [128, 512], F32)).ap()
                 for i in range(8)]

        p = Prog(nc, ctx)
        S = Cols(stats, 256)

        def settle(tok):
            return p.emit("scalar", lambda sc_: sc_.nop(), wait=[tok], inc="sc")
        R = Cols(rows, 5 * HALF + 128 + D)
        s2row = R("s2row", D)

        t_id = p.emit("scalar", lambda s: s.dma_start(out=ident, in_=ident_in),
                      inc="ld_id", dma=True)
        t_idf = p.emit("vector", lambda v: v.tensor_copy(identf, ident),
                       wait=[t_id], inc="ve")
        p.emit("vector", lambda v: v.memset(onescol, 1.0))
        p.emit("vector", lambda v: v.memset(onescf, 1.0))
        onesrow = R("onesrow", 128)
        p.emit("vector", lambda v: v.memset(onesrow, 1.0))
        p.emit("vector", lambda v: v.memset(gval, 0.0))
        p.emit("vector", lambda v: v.memset(econ[:, 0:1], EPS1))
        p.emit("vector", lambda v: v.memset(econ[:, 1:2], EPS2))
        p.emit("vector", lambda v: v.memset(econ[:, 2:3], RSQ_HD))
        t_ones = p.emit("vector", lambda v: v.nop(), inc="ve")
        t_x = p.emit("scalar", lambda s: s.dma_start(
            out=x_tm.rearrange("p (j d) -> p j d", d=D),
            in_=x_in.rearrange("(j p) d -> p j d", p=128)), inc="ld_x", dma=True)

        def bcast11(src11, dst_col, bk, wait=()):
            tm = p.pe_write([bk], lambda t: t.matmul(
                pbank[bk][:, 0:1], onesrow, src11, start=True, stop=True),
                wait=list(wait) + [t_ones])
            return p.bank_read("vector", [bk],
                               lambda v: v.tensor_copy(dst_col, pbank[bk][:, 0:1]))

        wrot = Rot([wbuf[:, 0:WT], wbuf[:, WT:2 * WT]])

        def gamma_pass(tiles, sum_col, bk, gi, ti, ni):
            for i, ap_ in enumerate(tiles):
                buf, war = wrot.get(i)
                tl = p.emit("sync", lambda s, buf=buf, ap_=ap_: s.dma_start(
                    out=buf, in_=ap_), wait=war, inc=f"wld{i % 2}", dma=True)
                tr = p.emit("vector", lambda v, buf=buf, i=i: v.reduce_sum(
                    gacc[:, i:i + 1], buf, axis=AX, apply_absolute_value=True),
                    wait=[tl], inc="ve")
                wrot.used(i, tr)
            tset = settle(tr)
            trs = p.emit("vector", lambda v: v.reduce_sum(
                sum_col, gacc[:, 0:len(tiles)], axis=AX),
                wait=[tset], inc="ve")
            tm = p.pe_write([bk], lambda t: t.matmul(
                pbank[bk][0:1, 0:1], sum_col, onescf, start=True, stop=True),
                wait=[trs, t_ones])
            tg = p.bank_read("vector", [bk], lambda v: v.tensor_scalar_mul(
                gval[0:1, gi:gi + 1], pbank[bk][0:1, 0:1], 1.0 / (O * D)))
            tg2 = settle(tg)
            tt = p.emit("vector", lambda v: v.tensor_scalar(
                gval[0:1, ti:ti + 1], gval[0:1, gi:gi + 1], TEPS, 0.5,
                ALU.add, ALU.mult), wait=[tg2], inc="ve")
            tn = p.emit("vector", lambda v: v.tensor_scalar(
                gval[0:1, ni:ni + 1], gval[0:1, gi:gi + 1], TEPS, -0.5,
                ALU.add, ALU.mult), wait=[tg2], inc="ve")
            return tg, settle(tt), settle(tn)

        fc_tiles = [w_fc[i * 128:(i + 1) * 128, dh * WT:(dh + 1) * WT]
                    for i in range(OC) for dh in range(2)]
        pr_tiles = [w_proj[i * 128:(i + 1) * 128, q * WT:(q + 1) * WT]
                    for i in range(DC) for q in range(8)]

        tg_fc, tt_fc, tn_fc = gamma_pass(fc_tiles, S("gs_fc"), 6, 0, 1, 6)
        t_gfc = bcast11(gval[0:1, 0:1], gcol[:, 0:1], 6, wait=[tg_fc])
        t_tfc0 = bcast11(gval[0:1, 1:2], gcol[:, 1:2], 6, wait=[tt_fc])
        t_tfc1 = bcast11(gval[0:1, 6:7], gcol[:, 4:5], 6, wait=[tn_fc])
        t_tfc = settle(t_tfc1)

        def ln_stage(tag, dep, double, scale_col, scale_dep):
            cs = {n: [S(f"{tag}{n}{j}") for j in range(NJ)]
                  for n in ("s", "q", "m", "v", "t", "w", "r", "a", "b")}
            t1 = [p.emit("vector", lambda v, j=j: v.reduce_sum(
                cs["s"][j], x_tm[:, j * D:(j + 1) * D], axis=AX),
                wait=dep, inc="ve") for j in range(NJ)][-1]
            t2 = [p.emit("scalar", lambda s, j=j: s.activation(
                sqf, x_tm[:, j * D:(j + 1) * D], AF.Square,
                accum_out=cs["q"][j]), wait=dep, inc="sc")
                for j in range(NJ)][-1]
            for j in range(NJ):
                p.emit("vector", lambda v, j=j: v.tensor_scalar_mul(
                    cs["m"][j], cs["s"][j], 1.0 / D), wait=[t1])
            for j in range(NJ):
                p.emit("vector", lambda v, j=j: v.tensor_scalar_mul(
                    cs["v"][j], cs["q"][j], 1.0 / D), wait=[t2])
            for j in range(NJ):
                p.emit("vector", lambda v, j=j: v.tensor_mul(
                    cs["t"][j], cs["m"][j], cs["m"][j]))
            tv = [p.emit("vector", lambda v, j=j: v.tensor_sub(
                cs["v"][j], cs["v"][j], cs["t"][j]), inc="ve")
                for j in range(NJ)][-1]
            if double:
                for j in range(NJ):
                    p.emit("vector", lambda v, j=j: v.tensor_scalar_add(
                        cs["w"][j], cs["v"][j], EPS1), wait=[tv])
                tw = [p.emit("vector", lambda v, j=j: v.scalar_tensor_tensor(
                    cs["w"][j], cs["w"][j], EPS2, cs["v"][j],
                    ALU.mult, ALU.add), inc="ve") for j in range(NJ)][-1]
                ts = [p.emit("scalar", lambda s, j=j: s.activation(
                    cs["w"][j], cs["w"][j], AF.Sqrt), wait=[tw], inc="sc")
                    for j in range(NJ)][-1]
            else:
                ts = [p.emit("scalar", lambda s, j=j: s.activation(
                    cs["w"][j], cs["v"][j], AF.Sqrt, bias=econ[:, 0:1]),
                    wait=[tv, t_ones], inc="sc") for j in range(NJ)][-1]
            tr = [p.emit("vector", lambda v, j=j: v.reciprocal(
                cs["r"][j], cs["w"][j]), wait=[ts], inc="ve")
                for j in range(NJ)][-1]
            for j in range(NJ):
                if scale_col is not None:
                    p.emit("vector", lambda v, j=j: v.tensor_mul(
                        cs["a"][j], cs["r"][j], scale_col),
                        wait=[tr] + list(scale_dep))
                else:
                    p.emit("vector", lambda v, j=j: v.tensor_copy(
                        cs["a"][j], cs["r"][j]), wait=[tr] + list(scale_dep))
            tb = [p.emit("vector", lambda v, j=j: v.scalar_tensor_tensor(
                cs["b"][j], cs["m"][j], -1.0, cs["a"][j],
                ALU.mult, ALU.mult), inc="ve") for j in range(NJ)][-1]
            return cs, settle(tb)

        xn_rot = Rot([xn_tm[:, 0:D], xn_tm[:, D:2 * D]])

        def transpose128(src, dst, identity, bk, off, dep):
            view = pbank[bk][:, off:off + 128] if identity is ident else \
                pbank[bk].bitcast(F32)[:, off:off + 128]
            tt = p.pe_write([bk], lambda t: t.transpose(view, src, identity),
                            wait=list(dep) + [t_id])
            te = p.bank_read("vector", [bk],
                             lambda v: v.tensor_copy(dst, view))
            return te

        pbank_bf = [b.bitcast(BF16) for b in pbank]

        def transpose_bf(src, dst, bk, slot2, dep):
            view = pbank_bf[bk][:, slot2 * 128:slot2 * 128 + 128]
            tt = p.pe_write([bk], lambda t: t.transpose(view, src, ident),
                            wait=list(dep) + [t_id])
            return p.bank_read("vector", [bk],
                               lambda v: v.tensor_copy(dst, view))

        def ln_into_fm(tag, dep, double, scale_col=None, scale_dep=()):
            cs, tab = ln_stage(tag, dep, double, scale_col, scale_dep)
            toks = []
            for j in range(NJ):
                dst, war = xn_rot.get(j)
                tn = p.emit("scalar", lambda s, j=j, dst=dst: s.activation(
                    dst, x_tm[:, j * D:(j + 1) * D], AF.Identity,
                    bias=cs["b"][j], scale=cs["a"][j]),
                    wait=[tab] + war, inc="sc")
                last = None
                for dc in range(DC):
                    last = transpose_bf(
                        dst[:, dc * 128:dc * 128 + 128],
                        xn_fm[:, dc * T + j * 128: dc * T + j * 128 + 128],
                        bk=dc % 2, slot2=(dc // 2) % 2, dep=[tn])
                xn_rot.used(j, last)
                toks.append(last)
            return toks

        gelu_toks = {0: [], 1: []}

        if "s1" in parts:
            t_ln1 = ln_into_fm("L1", [t_x], double=True,
                               scale_col=gcol[:, 0:1], scale_dep=[t_gfc])

            tnat_rot = Rot([tnat[:, 0:WT], tnat[:, WT:2 * WT]])
            sgn_rot = Rot([sgn[:, 0:WT]])
            tT_rot = Rot([tT[:, 0:DC * 128], tT[:, DC * 128:2 * DC * 128]])

            def tern_sub(i, src_ap, thr_col, thr_dep, tv_, dc0, war_t):
                pthr, nthr = thr_col
                buf, war_b = wrot.get(i)
                nat, war_n = tnat_rot.get(i)
                sg, war_s = sgn_rot.get(i)
                tl = p.emit("sync", lambda s, buf=buf, src_ap=src_ap:
                            s.dma_start(out=buf, in_=src_ap),
                            wait=war_b, inc=f"wld{i % 2}", dma=True)
                tm = p.emit("vector", lambda v, nat=nat, buf=buf, pthr=pthr:
                            v.tensor_single_scalar(nat, buf, pthr, ALU.is_ge),
                            wait=[tl] + war_n + list(thr_dep), inc="ve")
                tg = p.emit("vector", lambda v, sg=sg, buf=buf, nthr=nthr:
                            v.tensor_single_scalar(sg, buf, nthr, ALU.is_le),
                            wait=[tl] + war_s, inc="ve")
                tmu = p.emit("vector", lambda v, nat=nat, sg=sg:
                             v.tensor_sub(nat, nat, sg),
                             wait=[tg, tm], inc="ve")
                wrot.used(i, tmu)
                sgn_rot.used(i, tmu)
                ev = None
                for k in range(8):
                    dc = dc0 + k
                    ev = transpose_bf(
                        nat[:, k * 128:k * 128 + 128],
                        tv_[:, dc * 128:dc * 128 + 128],
                        bk=4 + dc % 2, slot2=(dc // 2) % 2,
                        dep=[tmu] + list(war_t))
                tnat_rot.used(i, ev)
                return ev

            tern_i = [0]

            def tern_group(gidx, subs, thr_col, thr_dep, scr_dst, scw_sem):
                tv_, war_t = tT_rot.get(gidx)
                ev = None
                for si, src_ap in enumerate(subs):
                    ev = tern_sub(tern_i[0], src_ap, thr_col, thr_dep,
                                  tv_, si * 8, war_t if si == 0 else [])
                    tern_i[0] += 1
                ncol = len(subs) * 8 * 128
                td = p.emit("gpsimd", lambda g, tv_=tv_, scr_dst=scr_dst,
                            ncol=ncol: g.dma_start(
                                out=scr_dst,
                                in_=tv_[:, 0:ncol].rearrange(
                                    "p (k c) -> p k c", c=128)),
                            wait=[ev], inc=scw_sem, dma=True)
                return tv_, ev, td

            def mm1_oc(oc, half, lhsT, deps, bk):
                tok = None
                for dc in range(DC):
                    if dc == 0:
                        tok = p.pe_write([bk], lambda t, dc=dc: t.matmul(
                            pbank[bk], lhsT[:, dc * 128:dc * 128 + 128],
                            xn_fm[:, dc * T + half * HALF:
                                  dc * T + half * HALF + HALF],
                            start=True, stop=False), wait=deps, inc=None)
                    else:
                        tok = p.pe_acc([bk], lambda t, dc=dc: t.matmul(
                            pbank[bk], lhsT[:, dc * 128:dc * 128 + 128],
                            xn_fm[:, dc * T + half * HALF:
                                  dc * T + half * HALF + HALF],
                            start=False, stop=(dc == DC - 1)),
                            inc="pe" if dc == DC - 1 else None)
                return tok

            hsq_rot = Rot([hsq[:, 0:HALF], hsq[:, HALF:2 * HALF]])

            def hstats_oc(oc, half, th, extra67=()):
                first, last = (oc == 0), (oc == OC - 1)
                hb, war = hsq_rot.get(oc)
                hv = h_fm[:, oc * HALF:(oc + 1) * HALF]
                tq = p.emit("vector", lambda v, hb=hb, hv=hv:
                            v.tensor_mul(hb, hv, hv),
                            wait=[th] + war, inc="ve")
                if first:
                    t1 = p.pe_write([6], lambda t, hv=hv: t.matmul(
                        pbank[6][0:1, 0:HALF], onescol, hv,
                        start=True, stop=False),
                        wait=[th, t_ones] + list(extra67), inc=None)
                    t2 = p.pe_write([7], lambda t, hb=hb: t.matmul(
                        pbank[7][0:1, 0:HALF], onescol, hb,
                        start=True, stop=False),
                        wait=[tq] + list(extra67), inc=None)
                else:
                    t1 = p.pe_acc([6], lambda t, hv=hv: t.matmul(
                        pbank[6][0:1, 0:HALF], onescol, hv,
                        start=False, stop=last),
                        wait=[th], inc="pe")
                    t2 = p.pe_acc([7], lambda t, hb=hb: t.matmul(
                        pbank[7][0:1, 0:HALF], onescol, hb,
                        start=False, stop=last),
                        wait=[tq], inc="pe")
                hsq_rot.used(oc, tq, t2)
                return t1, t2

            def extract_hstats(t_sum, t_sq, row_war=()):
                m2 = R("m2", HALF)
                v2 = R("v2", HALF)
                tm2 = p.bank_read("vector", [6], lambda v: v.tensor_scalar_mul(
                    m2, pbank[6][0:1, 0:HALF], 1.0 / O),
                    wait=[t_sum] + list(row_war))
                tv2 = p.bank_read("vector", [7], lambda v: v.tensor_scalar_mul(
                    v2, pbank[7][0:1, 0:HALF], 1.0 / O), wait=[t_sq])
                return settle(tv2)

            def finish_hstats(ts1, t_gprh, tr_dep):
                m2 = R("m2", HALF)
                v2 = R("v2", HALF)
                a2 = R("a2", HALF)
                w2 = R("w2", HALF)
                nm = R("nm", HALF)
                tnm = p.emit("vector", lambda v: v.tensor_scalar_mul(
                    nm, m2, -1.0), wait=[ts1], inc="ve")
                tms = p.emit("vector", lambda v: v.scalar_tensor_tensor(
                    w2, m2, 1.0, m2, ALU.mult, ALU.mult), wait=[ts1], inc="ve")
                ts2 = settle(tms)
                t_nm = p.emit("vector", lambda v: v.tensor_sub(v2, v2, w2),
                              wait=[ts2], inc="ve")
                ts = p.emit("scalar", lambda s: s.activation(
                    a2, v2, AF.Sqrt, bias=econ[0:1, 1:2]),
                    wait=[t_nm, t_ones], inc="sc")
                trc = p.emit("vector", lambda v: v.reciprocal(w2, a2),
                             wait=[ts], inc="ve")
                ts3 = settle(trc)
                ta = settle(p.emit("vector", lambda v: v.tensor_scalar_mul(
                    a2, w2, gval[0:1, 4:5]), wait=[ts3, t_gprh], inc="ve"))
                a2c = S("a2c", NJH)
                tc = None
                for u in range(NJH):
                    tt = p.pe_write([4], lambda t, u=u: t.transpose(
                        pbank[4][:, 300:301],
                        a2[0:1, u * 128:(u + 1) * 128], identf[0:1, 0:1]),
                        wait=[ta, t_idf] + list(tr_dep))
                    tc = p.bank_read("vector", [4], lambda v, u=u: v.tensor_copy(
                        a2c[:, u:u + 1], pbank[4][:, 300:301]))
                return a2c, nm, settle(tnm), settle(tc)

            t_scr_fc = []
            hs_last = None
            for oc in range(OC):
                tv_, trdy, tdma = tern_group(
                    oc, [fc_tiles[2 * oc], fc_tiles[2 * oc + 1]],
                    (gcol[:, 1:2], gcol[:, 4:5]), [t_tfc, settle(t_tfc0)],
                    tfc_out[:, oc * 128:(oc + 1) * 128].rearrange(
                        "(dc p) c -> p dc c", p=128), "scwf")
                t_scr_fc.append(tdma)
                bk = 2 + oc % 2
                tmm = mm1_oc(oc, 0, tv_, [trdy, t_ln1[-1]], bk)
                tT_rot.used(oc, tmm, tdma)
                tg = p.bank_read("scalar", [bk], lambda s, oc=oc, bk=bk:
                                 s.activation(h_fm[:, oc * HALF:(oc + 1) * HALF],
                                              pbank[bk], AF.Gelu),
                                 wait=[tmm])
                gelu_toks[0].append(tg)
            for oc in range(OC):
                hs_last = hstats_oc(oc, 0, gelu_toks[0][-1])

            t_exA = extract_hstats(hs_last[0], hs_last[1])

            tg_pr, tt_pr, tn_pr = gamma_pass(pr_tiles, S("gs_pr"), 4, 2, 3, 7)
            t_tpr_b0 = bcast11(gval[0:1, 3:4], gcol[:, 3:4], 4, wait=[tt_pr])
            t_tpr_b1 = bcast11(gval[0:1, 7:8], gcol[:, 5:6], 4, wait=[tn_pr])
            t_tpr_b = settle(t_tpr_b1)
            t_tpr_s0 = settle(t_tpr_b0)
            p.emit("vector", lambda v: v.tensor_scalar_mul(
                gval[0:1, 4:5], gval[0:1, 2:3], 0.5), wait=[tg_pr])
            t_gprh = p.emit("vector", lambda v: v.nop(), inc="ve")
            p.emit("scalar", lambda s: s.dma_start(
                out=gam_out, in_=gval), wait=[t_gprh, t_tpr_b],
                inc="st", dma=True)

            t_scr_pr = []
            for t2 in range(DC):
                for q8 in range(8):
                    g = OC + t2 * 8 + q8
                    src = w_proj[t2 * 128:(t2 + 1) * 128,
                                 q8 * WT:(q8 + 1) * WT]
                    dst = tpr_out[q8 * 8 * 128:(q8 + 1) * 8 * 128,
                                  t2 * 128:(t2 + 1) * 128].rearrange(
                                      "(k p) c -> p k c", p=128)
                    _, _, tdma = tern_group(g, [src],
                                            (gcol[:, 3:4], gcol[:, 5:6]),
                                            [t_tpr_b, t_tpr_s0], dst, "scwp")
                    t_scr_pr.append(tdma)
            t_tpr_done = ("scwp", p.counts["scwp"])
            t_tfc_all = ("scwf", p.counts["scwf"])

            a2cA, nmA, t_nmA, t_a2cA = finish_hstats(
                t_exA, t_gprh, [t_tpr_done])
            dbg_rw_tok = None
            if "dbg" in parts:
                p.emit("scalar", lambda sc_: sc_.dma_start(
                    out=dbg_hA, in_=h_fm), wait=[gelu_toks[0][-1]],
                    inc="st", dma=True)
                dbg_rw_tok = p.emit("scalar", lambda sc_: sc_.dma_start(
                    out=dbg_rwA, in_=rows), wait=[t_a2cA], inc="dbgst",
                    dma=True)

            w2_rot = Rot([w2b[:, 0:512], w2b[:, 512:1024]])
            s2_done = [None]

            def mm2_half(half, a2c, nm, t_nm, t_a2c, s2_acc):
                ev_toks = []
                for ot in range(4):
                    tok_mm = [None] * NJH
                    ts2_tok = None
                    for oc in range(OC):
                        wb, war = w2_rot.get(ot * OC + oc)
                        tl = p.emit("sync", lambda s, oc=oc, ot=ot, wb=wb:
                                    s.dma_start(out=wb, in_=tpr_out[
                                        oc * 128:(oc + 1) * 128,
                                        ot * 512:(ot + 1) * 512]),
                                    wait=war + [t_tpr_done],
                                    inc=f"w2l{(ot * OC + oc) % 2}", dma=True)
                        for jj in range(NJH):
                            lh = h_fm[:, oc * HALF + jj * 128:
                                      oc * HALF + jj * 128 + 128]
                            if oc == 0:
                                tok_mm[jj] = p.pe_write(
                                    [jj], lambda t, lh=lh, wb=wb, jj=jj:
                                    t.matmul(pbank[jj], lh, wb,
                                             start=True, stop=False),
                                    wait=[tl, gelu_toks[half][-1]],
                                    inc="pe" if jj == NJH - 1 else None)
                            else:
                                tok_mm[jj] = p.pe_acc(
                                    [jj], lambda t, lh=lh, wb=wb, jj=jj:
                                    t.matmul(pbank[jj], lh, wb,
                                             start=False, stop=False),
                                    wait=[tl],
                                    inc="pe" if (oc == OC - 1 or jj == NJH - 1)
                                    else None)
                        if s2_acc:
                            if oc == 0:
                                ts2_tok = p.pe_write(
                                    [5], lambda t, wb=wb: t.matmul(
                                        pbank[5][0:1, 0:512], onescol, wb,
                                        start=True, stop=False),
                                    wait=[t_ones, tl], inc="pe")
                            else:
                                ts2_tok = p.pe_acc([5], lambda t, wb=wb, oc=oc:
                                                   t.matmul(
                                    pbank[5][0:1, 0:512], onescol, wb,
                                    start=False, stop=(oc == OC - 1)),
                                    wait=[tl], inc="pe")
                            w2_rot.used(ot * OC + oc, tok_mm[NJH - 1], ts2_tok)
                        else:
                            w2_rot.used(ot * OC + oc, tok_mm[NJH - 1])
                    if s2_acc:
                        ts2 = settle(p.bank_read("vector", [5],
                                     lambda v, ot=ot: v.tensor_copy(
                            s2row[0:1, ot * 512:(ot + 1) * 512],
                            pbank[5][0:1, 0:512]), wait=[ts2_tok]))
                    else:
                        ts2 = s2_done[0]
                    for jj in range(NJH):
                        j = half * NJH + jj
                        tr1 = p.pe_acc([jj], lambda t, jj=jj, ot=ot: t.matmul(
                            pbank[jj], nm[0:1, jj * 128:(jj + 1) * 128],
                            s2row[0:1, ot * 512:(ot + 1) * 512],
                            start=False, stop=True),
                            wait=[tok_mm[jj], t_nm, ts2], inc="pe")
                        xv = x_tm[:, j * D + ot * 512: j * D + ot * 512 + 512]
                        te = p.bank_read(
                            "vector", [jj],
                            lambda v, jj=jj, xv=xv: v.scalar_tensor_tensor(
                                xv, pbank[jj], a2c[:, jj:jj + 1], xv,
                                ALU.mult, ALU.add),
                            wait=[tr1, t_a2c])
                        ev_toks.append(te)
                return ev_toks

            evA = mm2_half(0, a2cA, nmA, t_nmA, t_a2cA, s2_acc=True)
            s2_done[0] = p.emit("vector", lambda v: v.nop(), wait=[evA[-1]],
                                inc="ve")
            p.emit("scalar", lambda s: s.dma_start(out=s2_out, in_=s2row),
                   wait=[s2_done[0]], inc="st", dma=True)

            hs_last = None
            for oc in range(OC):
                tv_, war = tT_rot.get(oc)
                tl = p.emit("sync", lambda s, oc=oc, tv_=tv_: s.dma_start(
                    out=tv_.rearrange("p (dc c) -> p dc c", c=128),
                    in_=tfc_out[:, oc * 128:(oc + 1) * 128].rearrange(
                        "(dc p) c -> p dc c", p=128)),
                    wait=war + [t_tfc_all], inc=f"tld{oc % 2}", dma=True)
                bk = 2 + oc % 2
                tmm = mm1_oc(oc, 1, tv_, [tl, t_ln1[-1]], bk)
                tT_rot.used(oc, tmm)
                tg = p.bank_read("scalar", [bk], lambda s, oc=oc, bk=bk:
                                 s.activation(h_fm[:, oc * HALF:(oc + 1) * HALF],
                                              pbank[bk], AF.Gelu),
                                 wait=[tmm])
                gelu_toks[1].append(tg)
                hs_last = hstats_oc(oc, 1, tg,
                                    extra67=[t_nmA] if oc == 0 else ())
            t_exB = extract_hstats(
                hs_last[0], hs_last[1],
                row_war=[evA[-1], dbg_rw_tok] if "dbg" in parts
                else [evA[-1]])
            a2cB, nmB, t_nmB, t_a2cB = finish_hstats(t_exB, t_gprh, [])
            evB = mm2_half(1, a2cB, nmB, t_nmB, t_a2cB, s2_acc=False)

            t_s1_done = evB[-1]
            t_x1 = p.emit("scalar", lambda s: s.dma_start(
                out=x1_out.rearrange("(j p) d -> p j d", p=128),
                in_=x_tm.rearrange("p (j d) -> p j d", d=D)),
                wait=[t_s1_done], inc="st", dma=True)
        else:
            t_s1_done = t_x
            p.emit("scalar", lambda s: s.dma_start(
                out=x1_out.rearrange("(j p) d -> p j d", p=128),
                in_=x_tm.rearrange("p (j d) -> p j d", d=D)),
                wait=[t_x], inc="st", dma=True)

        if "qkv" in parts:
            t_ln2 = ln_into_fm("L2", [t_s1_done], double=False)

            qkv_rot = Rot([qkvw[:, 0:DC * 128], qkvw[:, DC * 128:2 * DC * 128]])
            qk_rot = Rot([qkst[:, 0:T], qkst[:, T:2 * T]])
            if "s1" in parts:
                qkv_rot.seed(t_s1_done)
                qk_rot.seed(t_s1_done)
            qk_last_ev = None
            for io in range(32):
                oc = io
                col0 = oc * 128
                wv_, war = qkv_rot.get(io)
                tl = p.emit("gpsimd", lambda g, col0=col0, wv_=wv_: g.dma_start(
                    out=wv_.rearrange("p (dc c) -> p dc c", c=128),
                    in_=w_qkv[:, col0:col0 + 128].rearrange(
                        "(dc p) c -> p dc c", p=128)),
                    wait=war, inc=f"qvl{io % 2}", dma=True)
                st_, swar = qk_rot.get(io)
                ev_toks = []
                for qt in range(2):
                    bk = (2 * io + qt) % 4
                    tok = None
                    for dc in range(DC):
                        if dc == 0:
                            tok = p.pe_write([bk], lambda t, dc=dc, qt=qt,
                                             wv_=wv_, bk=bk: t.matmul(
                                pbank[bk], wv_[:, dc * 128:dc * 128 + 128],
                                xn_fm[:, dc * T + qt * 512:
                                      dc * T + qt * 512 + 512],
                                start=True, stop=False),
                                wait=[tl, t_ln2[-1]], inc=None)
                        else:
                            tok = p.pe_acc([bk], lambda t, dc=dc, qt=qt,
                                           wv_=wv_, bk=bk: t.matmul(
                                pbank[bk], wv_[:, dc * 128:dc * 128 + 128],
                                xn_fm[:, dc * T + qt * 512:
                                      dc * T + qt * 512 + 512],
                                start=False, stop=(dc == DC - 1)),
                                inc="pe" if dc == DC - 1 else None)
                    is_q = oc < 16
                    if is_q:
                        te = p.bank_read("scalar", [bk],
                                         lambda s, st_=st_, qt=qt, bk=bk:
                                         s.activation(
                            st_[:, qt * 512:qt * 512 + 512], pbank[bk],
                            AF.Identity, scale=econ[:, 2:3]),
                            wait=swar + [t_ones])
                    else:
                        te = p.bank_read("scalar", [bk],
                                         lambda s, st_=st_, qt=qt, bk=bk:
                                         s.copy(
                            st_[:, qt * 512:qt * 512 + 512], pbank[bk]),
                            wait=swar)
                    ev_toks.append(te)
                dst = (q_out if oc < 16 else k_out)
                r0 = (oc % 16) * 128
                td = p.emit("scalar", lambda s, dst=dst, r0=r0, st_=st_:
                            s.dma_start(out=dst[r0:r0 + 128, :], in_=st_),
                            wait=ev_toks, inc="st", dma=True)
                qkv_rot.used(io, ev_toks[-1])
                qk_rot.used(io, td)
                qk_last_ev = ev_toks[-1]

            vw_rot = Rot([w2b[:, 0:512], w2b[:, 512:1024]])
            if "s1" in parts:
                vw_rot.seed(t_s1_done)
            vst_rot = Rot([vst[:, 0:512], vst[:, 512:1024]])
            if "s1" in parts:
                vst_rot.seed(t_s1_done)
            vi = 0
            for ot in range(4):
                tok_v = [None] * NJ
                for dc in range(DC):
                    wb, war = vw_rot.get(ot * DC + dc)
                    tl = p.emit("gpsimd", lambda g, dc=dc, ot=ot, wb=wb:
                                g.dma_start(out=wb, in_=w_qkv[
                                    dc * 128:(dc + 1) * 128,
                                    2 * D + ot * 512: 2 * D + ot * 512 + 512]),
                                wait=war, inc=f"vwl{(ot * DC + dc) % 2}",
                                dma=True)
                    for j in range(NJ):
                        lh = xn_fm[:, dc * T + j * 128: dc * T + j * 128 + 128]
                        if dc == 0:
                            tok_v[j] = p.pe_write(
                                [j], lambda t, lh=lh, wb=wb, j=j: t.matmul(
                                    pbank[j], lh, wb, start=True, stop=False),
                                wait=[tl, t_ln2[-1], qk_last_ev],
                                inc="pe" if j == NJ - 1 else None)
                        else:
                            tok_v[j] = p.pe_acc(
                                [j], lambda t, lh=lh, wb=wb, j=j, dc=dc:
                                t.matmul(
                                    pbank[j], lh, wb, start=False,
                                    stop=(dc == DC - 1)),
                                wait=[tl],
                                inc="pe" if (dc == DC - 1 or j == NJ - 1)
                                else None)
                    vw_rot.used(ot * DC + dc, tok_v[NJ - 1])
                for j in range(NJ):
                    sb, swar = vst_rot.get(vi)
                    te = p.bank_read("scalar", [j], lambda s, sb=sb, j=j:
                                     s.activation(sb, pbank[j], AF.Copy),
                                     wait=[tok_v[j]] + swar)
                    td = p.emit("gpsimd", lambda g, sb=sb, j=j, ot=ot:
                                g.dma_start(out=v_out[
                                    j * 128:(j + 1) * 128,
                                    ot * 512:(ot + 1) * 512], in_=sb),
                                wait=[te], inc="stg", dma=True)
                    vst_rot.used(vi, td)
                    vi += 1

        if "dbg" in parts:
            last_all = p.emit("vector", lambda v: v.nop(), inc="ve")
            p.emit("scalar", lambda sc_: sc_.dma_start(out=dbg_xn, in_=xn_fm),
                   wait=[last_all], inc="st", dma=True)
            p.emit("scalar", lambda sc_: sc_.dma_start(out=dbg_h, in_=h_fm),
                   wait=[last_all], inc="st", dma=True)
            p.emit("scalar", lambda sc_: sc_.dma_start(out=dbg_st, in_=stats),
                   wait=[last_all], inc="st", dma=True)
            p.emit("scalar", lambda sc_: sc_.dma_start(out=dbg_rw, in_=rows),
                   wait=[last_all], inc="st", dma=True)
        for sname in ("st", "stg", "scwf", "scwp", "dbgst"):
            n = p.counts.get(sname, 0)
            if n:
                p.emit("sync", lambda s: s.nop(), wait=[(sname, n)])

        p.build()
    return nc


NKT = 64
NH = 16


def build_launch2(parts=("att", "proj", "s3", "s4", "s5", "s6")):
    nc = bass.Bass("TRN2", target_bir_lowering=False, debug=False,
                   detect_race_conditions=False)

    x1_in = nc.dram_tensor("x1", [T, D], F32, kind="ExternalInput").ap()
    q_in = nc.dram_tensor("q_fm", [D, T], BF16, kind="ExternalInput").ap()
    k_in = nc.dram_tensor("k_all", [D, SEQ], BF16, kind="ExternalInput").ap()
    v_in = nc.dram_tensor("v_all", [NH * 128, NKT * 128], BF16,
                          kind="ExternalInput").ap()
    ko_in = nc.dram_tensor("k_own", [D, T], BF16, kind="ExternalInput").ap()
    vo_in = nc.dram_tensor("v_own", [NH * 128, NJ * 128], BF16,
                           kind="ExternalInput").ap()
    moff_in = nc.dram_tensor("moff", [1, 8 * 128], BF16,
                             kind="ExternalInput").ap()
    mtri_in = nc.dram_tensor("mtri", [128, 128], BF16, kind="ExternalInput").ap()
    wo0_in = nc.dram_tensor("wo0", [4 * 128, NH * 512], F32,
                            kind="ExternalInput").ap()
    tfc_in = nc.dram_tensor("tfc", [OC * 128, DC * 128], BF16,
                            kind="ExternalInput").ap()
    tpr_in = nc.dram_tensor("tpr", [O, D], BF16, kind="ExternalInput").ap()
    gam_in = nc.dram_tensor("gam", [1, 8], F32, kind="ExternalInput").ap()
    s2_in = nc.dram_tensor("s2in", [1, D], F32, kind="ExternalInput").ap()
    wr_in = nc.dram_tensor("wr", [128, 16 * 32], F32, kind="ExternalInput").ap()
    u_in = nc.dram_tensor("uflat", [128, 16 * 512], F32,
                          kind="ExternalInput").ap()
    vf_in = nc.dram_tensor("vflat", [128, 4 * D], F32,
                           kind="ExternalInput").ap()
    at_in = nc.dram_tensor("attrT", [32, 32], F32, kind="ExternalInput").ap()
    wa_in = nc.dram_tensor("waCol", [32, 1], F32, kind="ExternalInput").ap()
    ea_in = nc.dram_tensor("eAll", [32, 512], BF16, kind="ExternalInput").ap()
    wv_in = nc.dram_tensor("wvT", [128, 16 * 512], F32,
                           kind="ExternalInput").ap()
    bv_in = nc.dram_tensor("bv", [4, 128], F32, kind="ExternalInput").ap()
    wo1_in = nc.dram_tensor("wo1", [D, D], F32, kind="ExternalInput").ap()
    ob_in = nc.dram_tensor("outb", [1, D], F32, kind="ExternalInput").ap()
    ident_in = nc.dram_tensor("ident", [128, 128], BF16,
                              kind="ExternalInput").ap()

    x_out = nc.dram_tensor("x_out", [T, D], F32, kind="ExternalOutput").ap()
    if "dbgm" in parts:
        dbg_moe = nc.dram_tensor("dbg_moe", [128, 4096], BF16,
                                 kind="ExternalOutput").ap()
        dbg_gt = nc.dram_tensor("dbg_gt", [32, 512], BF16,
                                kind="ExternalOutput").ap()
        dbg_gbf = nc.dram_tensor("dbg_gbf", [128, 256], BF16,
                                 kind="ExternalOutput").ap()
        dbg_usb = nc.dram_tensor("dbg_usb", [128, 8192], BF16,
                                 kind="ExternalOutput").ap()
        dbg_xnf = nc.dram_tensor("dbg_xnf", [128, DC * T], BF16,
                                 kind="ExternalOutput").ap()
    if "dbg" in parts:
        dbg_attn = nc.dram_tensor("dbg_attn", [128, NH * T], BF16,
                                  kind="ExternalOutput").ap()
        dbg_xn = nc.dram_tensor("dbg_xn", [128, DC * T], BF16,
                                kind="ExternalOutput").ap()
        dbg_den = nc.dram_tensor("dbg_den", [128, T], F32,
                                 kind="ExternalOutput").ap()

    ctx = contextlib.ExitStack()
    with ctx:
        E = ctx.enter_context
        x_tm = E(nc.sbuf_tensor("x_tm", [128, NJ * D], F32)).ap()
        arA = E(nc.sbuf_tensor("arA", [128, 32768], BF16)).ap()
        arB = E(nc.sbuf_tensor("arB", [128, 16384], BF16)).ap()
        msc = E(nc.sbuf_tensor("msc", [128, 10496], BF16)).ap()
        mscf = msc.bitcast(F32)
        stats = E(nc.sbuf_tensor("stats", [128, 1024], F32)).ap()
        rows = E(nc.sbuf_tensor("rows", [1, 4800], F32)).ap()
        moff_sb = E(nc.sbuf_tensor("moffsb", [1, 8 * 128], BF16)).ap()
        ones1b = E(nc.sbuf_tensor("ones1b", [1, 128], BF16)).ap()
        mtri_sb = E(nc.sbuf_tensor("mtrisb", [128, 128], BF16)).ap()
        ident = E(nc.sbuf_tensor("identsb", [128, 128], BF16)).ap()
        identf = E(nc.sbuf_tensor("identfsb", [128, 128], F32)).ap()
        onescol = E(nc.sbuf_tensor("onescol", [128, 1], BF16)).ap()
        onescf = E(nc.sbuf_tensor("onescf", [128, 1], F32)).ap()
        econ = E(nc.sbuf_tensor("econ", [128, 3], F32)).ap()
        gval = E(nc.sbuf_tensor("gval", [1, 8], F32)).ap()
        gcol = E(nc.sbuf_tensor("gcol", [128, 1], F32)).ap()
        attr_sb = E(nc.sbuf_tensor("attr_sb", [32, 32], F32)).ap()
        wa_sb = E(nc.sbuf_tensor("wa_sb", [32, 1], F32)).ap()
        awb = E(nc.sbuf_tensor("awb", [128, 32], F32)).ap()
        bvn = E(nc.sbuf_tensor("bvn", [4, 128], F32)).ap()
        bv_sb = E(nc.sbuf_tensor("bv_sb", [128, 4], F32)).ap()

        pb0 = E(nc.psum_tensor("pb0", [128, 512], F32)).ap()
        pb1 = E(nc.psum_tensor("pb1", [128, 512], F32)).ap()
        pbigA = E(nc.psum_tensor("pbigA", [128, 1024], F32)).ap()
        pbigB = E(nc.psum_tensor("pbigB", [128, 1024], F32)).ap()
        pb6 = E(nc.psum_tensor("pb6", [128, 512], F32)).ap()
        pb7 = E(nc.psum_tensor("pb7", [128, 512], F32)).ap()
        pb512 = {0: pb0, 1: pb1, 2: pbigA[:, 0:512], 3: pbigA[:, 512:1024],
                 4: pbigB[:, 0:512], 5: pbigB[:, 512:1024], 6: pb6, 7: pb7}
        pb_bf = {0: pb0.bitcast(BF16), 1: pb1.bitcast(BF16),
                 2: pbigA.bitcast(BF16), 4: pbigB.bitcast(BF16)}

        p = Prog(nc, ctx)
        S = Cols(stats, 1024)
        R = Cols(rows, 4800)

        def settle(tok):
            return p.emit("scalar", lambda s_: s_.nop(), wait=[tok], inc="sc")

        onesrow = R("onesrow", 128)
        s2row = R("s2row", D)
        m2 = R("m2", HALF)
        v2 = R("v2", HALF)
        a2 = R("a2", HALF)
        w2 = R("w2", HALF)
        nm = R("nm", HALF)
        recipA = m2[0:1, 0:512]
        recipB = v2[0:1, 0:512]
        aw_row = R("awrow", 32)
        sg_row = R("sgrow", 32)

        t_id = p.emit("scalar", lambda s: s.dma_start(out=ident, in_=ident_in),
                      inc="ld1", dma=True)
        p.emit("scalar", lambda s: s.dma_start(out=mtri_sb, in_=mtri_in),
               inc="ld1", dma=True)
        p.emit("scalar", lambda s: s.dma_start(out=moff_sb, in_=moff_in),
               inc="ld1", dma=True)
        p.emit("scalar", lambda s: s.dma_start(out=gval, in_=gam_in),
               inc="ld1", dma=True)
        p.emit("scalar", lambda s: s.dma_start(out=s2row, in_=s2_in),
               inc="ld1", dma=True)
        t_x1 = p.emit("scalar", lambda s: s.dma_start(
            out=x_tm.rearrange("p (j d) -> p j d", d=D),
            in_=x1_in.rearrange("(j p) d -> p j d", p=128)),
            inc="ld1", dma=True)
        t_ld1 = ("ld1", p.counts["ld1"])
        t_idf = p.emit("vector", lambda v: v.tensor_copy(identf, ident),
                       wait=[t_ld1], inc="ve")
        p.emit("vector", lambda v: v.memset(onescol, 1.0))
        p.emit("vector", lambda v: v.memset(onescf, 1.0))
        p.emit("vector", lambda v: v.memset(onesrow, 1.0))
        p.emit("vector", lambda v: v.memset(ones1b, 1.0))
        p.emit("vector", lambda v: v.memset(econ[:, 0:1], EPS1))
        p.emit("vector", lambda v: v.memset(econ[:, 1:2], EPS2))
        t_ones = p.emit("vector", lambda v: v.nop(), inc="ve")
        t_gb = p.pe_write([6], lambda t: t.matmul(
            pb6[:, 0:1], onesrow, gval[0:1, 0:1], start=True, stop=True),
            wait=[t_ones, t_ld1])
        t_gfc = p.bank_read("vector", [6],
                            lambda v: v.tensor_copy(gcol, pb6[:, 0:1]))

        attn_fm = arB
        kh_rot = Rot([arA[:, 0:8192], arA[:, 8192:16384]])
        vh_rot = Rot([arA[:, 16384:24576], arA[:, 24576:32768]])
        q_rot = Rot([msc[:, 0:1024], msc[:, 1024:2048]])
        ex_rot = Rot([msc[:, 2048:3072], msc[:, 3072:4096]])
        dg_rot = Rot([msc[:, 4096:4224], msc[:, 4224:4352]])
        ko_rot = Rot([msc[:, 4352:5376]])
        vo_rot = Rot([msc[:, 5376:6400]])
        denf = mscf[:, 3200:4224]
        recf = mscf[:, 4224:5248]
        den_war = [[]]
        recf_war = [[]]
        recrow_war = [[]]

        def attn_head(h):
            kh, kwar = kh_rot.get(h)
            t_k = p.emit("sync", lambda s, kh=kh, h=h: s.dma_start(
                out=kh, in_=k_in[h * 128:(h + 1) * 128, :]),
                wait=kwar, inc=f"khl{h % 2}", dma=True)
            qh, qwar = q_rot.get(h)
            t_q = p.emit("sync", lambda s, qh=qh, h=h: s.dma_start(
                out=qh, in_=q_in[h * 128:(h + 1) * 128, :]),
                wait=qwar, inc=f"qhl{h % 2}", dma=True)
            koh, kowar = ko_rot.get(h)
            t_ko = p.emit("sync", lambda s, koh=koh, h=h: s.dma_start(
                out=koh, in_=ko_in[h * 128:(h + 1) * 128, :]),
                wait=kowar, inc=f"kol{h % 2}", dma=True)
            vh, vwar = vh_rot.get(h)
            t_v = p.emit("gpsimd", lambda g, vh=vh, h=h: g.dma_start(
                out=vh, in_=v_in[h * 128:(h + 1) * 128, :]),
                wait=vwar, inc=f"vhl{h % 2}", dma=True)
            voh, vowar = vo_rot.get(h)
            t_vo = p.emit("gpsimd", lambda g, voh=voh, h=h: g.dma_start(
                out=voh, in_=vo_in[h * 128:(h + 1) * 128, :]),
                wait=vowar, inc=f"vol{h % 2}", dma=True)

            exp_toks = {}
            score_toks = []
            av_toks = []
            den_prev = [None]

            def emit_scores(kt):
                q0 = (kt >> 3) * 128
                grp = pbigA if kt % 2 == 0 else pbigB
                b0 = 2 if kt % 2 == 0 else 4
                toks = []
                banks = []
                if q0 < 512:
                    tk = p.pe_write([b0], lambda t, kt=kt, q0=q0, grp=grp:
                                    t.matmul(grp[:, q0:512],
                                             kh[:, kt * 128:(kt + 1) * 128],
                                             qh[:, q0:512],
                                             start=True, stop=False),
                                    wait=[t_k, t_q], inc=None)
                    tk = p.pe_acc([b0], lambda t, kt=kt, q0=q0, grp=grp:
                                  t.matmul(grp[:, q0:q0 + 128], ones1b,
                                           moff_sb[0:1, (kt % 8) * 128:
                                                   (kt % 8 + 1) * 128],
                                           start=False, stop=True),
                                  wait=[t_ld1, t_ones], inc="pe")
                    toks.append(tk)
                    banks.append(b0)
                a = max(q0, 512)
                ops = [lambda t, kt=kt, a=a, grp=grp: t.matmul(
                    grp[:, a:1024], kh[:, kt * 128:(kt + 1) * 128],
                    qh[:, a:1024], start=True, stop=(q0 < 512))]
                if q0 >= 512:
                    ops.append(lambda t, kt=kt, q0=q0, grp=grp: t.matmul(
                        grp[:, q0:q0 + 128], ones1b,
                        moff_sb[0:1, (kt % 8) * 128:(kt % 8 + 1) * 128],
                        start=False, stop=True))
                tk = p.pe_write([b0 + 1], ops[0], wait=[t_k, t_q, t_ld1],
                                inc=None if len(ops) > 1 else "pe")
                if len(ops) > 1:
                    tk = p.pe_acc([b0 + 1], ops[1], inc="pe")
                toks.append(tk)
                banks.append(b0 + 1)
                ex, exwar = ex_rot.get(kt)
                texp = p.bank_read("scalar", banks,
                                   lambda s, ex=ex, q0=q0, grp=grp:
                                   s.activation(ex[:, q0:1024],
                                                grp[:, q0:1024], AF.Exp),
                                   wait=toks + exwar)
                exp_toks[kt] = (texp, ex, q0)
                score_toks.extend(toks)
                if kt == 0:
                    td = p.emit("vector", lambda v, ex=ex: v.tensor_copy(
                        denf, ex), wait=[texp] + den_war[0], inc="ve")
                else:
                    td = p.emit("vector", lambda v, ex=ex, q0=q0:
                                v.tensor_add(denf[:, q0:1024],
                                             denf[:, q0:1024],
                                             ex[:, q0:1024]),
                                wait=[texp, den_prev[0]], inc="ve")
                den_prev[0] = td
                ex_rot.used(kt, td)

            def emit_av(kt):
                texp, ex, q0 = exp_toks.pop(kt)
                toks = []
                if q0 < 512:
                    fn = lambda t, kt=kt, q0=q0, ex=ex: t.matmul(
                        pb0[:, q0:512], vh[:, kt * 128:(kt + 1) * 128],
                        ex[:, q0:512], start=(kt == 0), stop=False)
                    if kt == 0:
                        tk = p.pe_write([0], fn, wait=[texp, t_v], inc=None)
                    else:
                        tk = p.pe_acc([0], fn, wait=[texp, t_v], inc=None)
                    toks.append(tk)
                a = max(q0, 512)
                fn = lambda t, kt=kt, a=a, ex=ex: t.matmul(
                    pb1[:, a - 512:512], vh[:, kt * 128:(kt + 1) * 128],
                    ex[:, a:1024], start=(kt == 0), stop=False)
                if kt == 0:
                    tk = p.pe_write([1], fn, wait=[texp, t_v], inc="pe")
                else:
                    tk = p.pe_acc([1], fn, wait=[texp, t_v], inc="pe")
                toks.append(tk)
                av_toks.extend(toks)
                ex_rot.used_add(kt, toks[-1])

            prev = None
            for kt in range(NKT):
                emit_scores(kt)
                if prev is not None:
                    emit_av(prev)
                prev = kt
            emit_av(prev)

            for m in range(NJ):
                grp = pbigA if m % 2 == 0 else pbigB
                b0 = 2 if m % 2 == 0 else 4
                p.pe_write([b0], lambda t, m=m, grp=grp: t.matmul(
                    grp[:, 0:128], koh[:, m * 128:(m + 1) * 128],
                    qh[:, m * 128:(m + 1) * 128], start=True, stop=False),
                    wait=[t_ko, t_q], inc=None)
                tk = p.pe_acc([b0], lambda t, grp=grp: t.matmul(
                    grp[:, 0:128], ident, mtri_sb, start=False, stop=True),
                    wait=[t_id, t_ld1], inc="pe")
                dg, dgwar = dg_rot.get(m)
                texp = p.bank_read("scalar", [b0],
                                   lambda s, dg=dg, grp=grp: s.activation(
                                       dg, grp[:, 0:128], AF.Exp),
                                   wait=[tk] + dgwar)
                td = p.emit("vector", lambda v, dg=dg, m=m: v.tensor_add(
                    denf[:, m * 128:(m + 1) * 128],
                    denf[:, m * 128:(m + 1) * 128], dg),
                    wait=[texp, den_prev[0]], inc="ve")
                den_prev[0] = td
                bk = 0 if m < 4 else 1
                pbv = pb0 if m < 4 else pb1
                off = (m * 128) % 512
                tav = p.pe_acc([bk], lambda t, pbv=pbv, off=off, m=m, dg=dg:
                               t.matmul(pbv[:, off:off + 128],
                                        voh[:, m * 128:(m + 1) * 128], dg,
                                        start=False, stop=(m % 4 == 3)),
                               wait=[texp, t_vo], inc="pe")
                av_toks.append(tav)
                dg_rot.used(m, td, tav)

            tredA = p.pe_write([6], lambda t: t.matmul(
                pb6[0:1, 0:512], onescf, denf[:, 0:512],
                start=True, stop=True), wait=[den_prev[0], t_ones])
            tredB = p.pe_write([7], lambda t: t.matmul(
                pb7[0:1, 0:512], onescf, denf[:, 512:1024],
                start=True, stop=True), wait=[den_prev[0], t_ones])
            den_war[0] = [tredA, tredB]
            trcA = p.bank_read("vector", [6], lambda v: v.reciprocal(
                recipA, pb6[0:1, 0:512]), wait=[tredA])
            trcB = p.bank_read("vector", [7], lambda v: v.reciprocal(
                recipB, pb7[0:1, 0:512]), wait=[tredB])
            tbA = p.pe_write([6], lambda t: t.matmul(
                pb6, onesrow, recipA, start=True, stop=True), wait=[trcA])
            tbB = p.pe_write([7], lambda t: t.matmul(
                pb7, onesrow, recipB, start=True, stop=True), wait=[trcB])
            recrow_war[0] = [tbA, tbB]
            tcA = p.bank_read("scalar", [6], lambda s: s.copy(
                recf[:, 0:512], pb6), wait=recf_war[0])
            tcB = p.bank_read("scalar", [7], lambda s: s.copy(
                recf[:, 512:1024], pb7), wait=recf_war[0])
            teA = p.bank_read("vector", [0], lambda v, h=h: v.tensor_mul(
                attn_fm[:, h * 1024:h * 1024 + 512], pb0, recf[:, 0:512]),
                wait=[tcA])
            teB = p.bank_read("vector", [1], lambda v, h=h: v.tensor_mul(
                attn_fm[:, h * 1024 + 512:h * 1024 + 1024], pb1,
                recf[:, 512:1024]), wait=[tcB])
            recf_war[0] = [teA, teB]
            kh_rot.used(h, *score_toks[-4:])
            q_rot.used(h, score_toks[-1], tk)
            ko_rot.used(h, tk)
            vh_rot.used(h, *av_toks[-4:])
            vo_rot.used(h, av_toks[-1])
            return teA, teB

        t_attn = t_x1
        attn_evacs = []
        if "att" in parts:
            for h in range(NH):
                teA, teB = attn_head(h)
                attn_evacs += [teA, teB]
            t_attn = attn_evacs[-1]
            if "dbg" in parts:
                p.emit("scalar", lambda s: s.dma_start(
                    out=dbg_attn, in_=attn_fm), wait=[t_attn],
                    inc="st", dma=True)
                p.emit("scalar", lambda s: s.dma_start(
                    out=dbg_den, in_=denf), wait=[t_attn],
                    inc="st", dma=True)

        t_res = t_x1
        if "proj" in parts:
            wo_rot = Rot([arA[:, 0:8192], arA[:, 8192:16384]])
            wo_rot.seed(*(kh_rot.last[0] + kh_rot.last[1] +
                          vh_rot.last[0] + vh_rot.last[1]))
            ev = None
            for ot in range(4):
                wsl, wwar = wo_rot.get(ot)
                tw = p.emit("gpsimd", lambda g, ot=ot, wsl=wsl: g.dma_start(
                    out=wsl, in_=wo0_in[ot * 128:(ot + 1) * 128, :]),
                    wait=wwar, inc=f"wol{ot % 2}", dma=True)
                mm_last = []
                for tj in range(NJ):
                    bk = tj % 4
                    tok = None
                    for hh in range(NH):
                        fn = lambda t, hh=hh, tj=tj, wsl=wsl, bk=bk: t.matmul(
                            pb512[bk], attn_fm[:, hh * 1024 + tj * 128:
                                               hh * 1024 + tj * 128 + 128],
                            wsl[:, hh * 512:(hh + 1) * 512],
                            start=(hh == 0), stop=(hh == NH - 1))
                        if hh == 0:
                            tok = p.pe_write([bk], fn,
                                             wait=[tw, t_attn], inc=None)
                        else:
                            tok = p.pe_acc([bk], fn,
                                           inc="pe" if hh == NH - 1 else None)
                    xv = x_tm[:, tj * D + ot * 512: tj * D + ot * 512 + 512]
                    ev = p.bank_read("vector", [bk],
                                     lambda v, xv=xv, bk=bk: v.tensor_add(
                                         xv, pb512[bk], xv),
                                     wait=[tok, t_x1])
                    mm_last.append(tok)
                wo_rot.used(ot, *mm_last)
            t_res = ev

        xn_fm = arB
        xn_tm_rot = Rot([msc[:, 6400:8448], msc[:, 8448:10496]])
        sqf = msc[:, 6400:8448]

        def ln_stage(tag, dep, double, scale_col, scale_dep):
            cs = {n: [S(f"{tag}{n}{j}") for j in range(NJ)]
                  for n in ("s", "q", "m", "v", "t", "w", "r", "a", "b")}
            t1 = [p.emit("vector", lambda v, j=j: v.reduce_sum(
                cs["s"][j], x_tm[:, j * D:(j + 1) * D], axis=AX),
                wait=dep, inc="ve") for j in range(NJ)][-1]
            t2 = [p.emit("scalar", lambda s, j=j: s.activation(
                sqf, x_tm[:, j * D:(j + 1) * D], AF.Square,
                accum_out=cs["q"][j]), wait=dep, inc="sc")
                for j in range(NJ)][-1]
            for j in range(NJ):
                p.emit("vector", lambda v, j=j: v.tensor_scalar_mul(
                    cs["m"][j], cs["s"][j], 1.0 / D), wait=[t1])
            for j in range(NJ):
                p.emit("vector", lambda v, j=j: v.tensor_scalar_mul(
                    cs["v"][j], cs["q"][j], 1.0 / D), wait=[t2])
            for j in range(NJ):
                p.emit("vector", lambda v, j=j: v.tensor_mul(
                    cs["t"][j], cs["m"][j], cs["m"][j]))
            tv = [p.emit("vector", lambda v, j=j: v.tensor_sub(
                cs["v"][j], cs["v"][j], cs["t"][j]), inc="ve")
                for j in range(NJ)][-1]
            if double:
                for j in range(NJ):
                    p.emit("vector", lambda v, j=j: v.tensor_scalar_add(
                        cs["w"][j], cs["v"][j], EPS1), wait=[tv])
                tw = [p.emit("vector", lambda v, j=j: v.scalar_tensor_tensor(
                    cs["w"][j], cs["w"][j], EPS2, cs["v"][j],
                    ALU.mult, ALU.add), inc="ve") for j in range(NJ)][-1]
                ts = [p.emit("scalar", lambda s, j=j: s.activation(
                    cs["w"][j], cs["w"][j], AF.Sqrt), wait=[tw], inc="sc")
                    for j in range(NJ)][-1]
            else:
                ts = [p.emit("scalar", lambda s, j=j: s.activation(
                    cs["w"][j], cs["v"][j], AF.Sqrt, bias=econ[:, 0:1]),
                    wait=[tv, t_ones], inc="sc") for j in range(NJ)][-1]
            tr = [p.emit("vector", lambda v, j=j: v.reciprocal(
                cs["r"][j], cs["w"][j]), wait=[ts], inc="ve")
                for j in range(NJ)][-1]
            for j in range(NJ):
                if scale_col is not None:
                    p.emit("vector", lambda v, j=j: v.tensor_mul(
                        cs["a"][j], cs["r"][j], scale_col),
                        wait=[tr] + list(scale_dep))
                else:
                    p.emit("vector", lambda v, j=j: v.tensor_copy(
                        cs["a"][j], cs["r"][j]), wait=[tr] + list(scale_dep))
            tb = [p.emit("vector", lambda v, j=j: v.scalar_tensor_tensor(
                cs["b"][j], cs["m"][j], -1.0, cs["a"][j],
                ALU.mult, ALU.mult), inc="ve") for j in range(NJ)][-1]
            return cs, settle(tb)

        def transpose_bf(src, dst, bk, slot2, dep):
            view = pb_bf[bk][:, slot2 * 128:slot2 * 128 + 128]
            p.pe_write([bk], lambda t: t.transpose(view, src, ident),
                       wait=list(dep) + [t_id])
            return p.bank_read("vector", [bk],
                               lambda v: v.tensor_copy(dst, view))

        def ln_into_fm(tag, dep, double, scale_col=None, scale_dep=(),
                       fm_war=()):
            cs, tab = ln_stage(tag, dep, double, scale_col, scale_dep)
            toks = []
            for j in range(NJ):
                dst, war = xn_tm_rot.get(j)
                tn = p.emit("scalar", lambda s, j=j, dst=dst: s.activation(
                    dst, x_tm[:, j * D:(j + 1) * D], AF.Identity,
                    bias=cs["b"][j], scale=cs["a"][j]),
                    wait=[tab] + war + (list(fm_war) if j == 0 else []),
                    inc="sc")
                last = None
                for dc in range(DC):
                    last = transpose_bf(
                        dst[:, dc * 128:dc * 128 + 128],
                        xn_fm[:, dc * T + j * 128: dc * T + j * 128 + 128],
                        bk=dc % 2, slot2=(dc // 2) % 2, dep=[tn])
                xn_tm_rot.used(j, last)
                toks.append(last)
            return toks

        if "s3" in parts:
            t_ln3 = ln_into_fm("L3", [t_res], double=True,
                               scale_col=gcol, scale_dep=[t_gfc],
                               fm_war=attn_evacs[-2:] if attn_evacs else [])
            tT_rot = Rot([msc[:, 0:2048], msc[:, 2048:4096]])
            tT_rot.seed(*(q_rot.last[0] + q_rot.last[1] +
                          ex_rot.last[0] + ex_rot.last[1]))
            w2_rot = Rot([msc[:, 4096:4608], msc[:, 4608:5120]])
            w2_rot.seed(*(dg_rot.last[0] + dg_rot.last[1] +
                          ko_rot.last[0]))
            hsq_rot = Rot([msc[:, 5120:5632], msc[:, 5632:6144]])
            hsq_rot.seed(*(ko_rot.last[0] + vo_rot.last[0]))
            h_fm = arA
            den_free = den_war[0] + recf_war[0]

            def mm1s3(half, oc, tv_, deps, bk):
                tok = None
                for dc in range(DC):
                    if dc == 0:
                        tok = p.pe_write([bk], lambda t, dc=dc: t.matmul(
                            pb512[bk], tv_[:, dc * 128:dc * 128 + 128],
                            xn_fm[:, dc * T + half * HALF:
                                  dc * T + half * HALF + HALF],
                            start=True, stop=False), wait=deps, inc=None)
                    else:
                        tok = p.pe_acc([bk], lambda t, dc=dc: t.matmul(
                            pb512[bk], tv_[:, dc * 128:dc * 128 + 128],
                            xn_fm[:, dc * T + half * HALF:
                                  dc * T + half * HALF + HALF],
                            start=False, stop=(dc == DC - 1)),
                            inc="pe" if dc == DC - 1 else None)
                return tok

            def hstats_oc(oc, th):
                first, last = (oc == 0), (oc == OC - 1)
                hb, war = hsq_rot.get(oc)
                hv = h_fm[:, oc * HALF:(oc + 1) * HALF]
                tq = p.emit("vector", lambda v, hb=hb, hv=hv:
                            v.tensor_mul(hb, hv, hv),
                            wait=[th] + war, inc="ve")
                if first:
                    t1 = p.pe_write([6], lambda t, hv=hv: t.matmul(
                        pb6[0:1, 0:HALF], onescol, hv,
                        start=True, stop=False),
                        wait=[th, t_ones] + den_free, inc=None)
                    t2 = p.pe_write([7], lambda t, hb=hb: t.matmul(
                        pb7[0:1, 0:HALF], onescol, hb,
                        start=True, stop=False), wait=[tq], inc=None)
                else:
                    t1 = p.pe_acc([6], lambda t, hv=hv: t.matmul(
                        pb6[0:1, 0:HALF], onescol, hv,
                        start=False, stop=last), wait=[th], inc="pe")
                    t2 = p.pe_acc([7], lambda t, hb=hb: t.matmul(
                        pb7[0:1, 0:HALF], onescol, hb,
                        start=False, stop=last), wait=[tq], inc="pe")
                hsq_rot.used(oc, tq, t2)
                return t1, t2

            def extract_hstats(t_sum, t_sq, row_war=()):
                tm2 = p.bank_read("vector", [6], lambda v: v.tensor_scalar_mul(
                    m2, pb6[0:1, 0:HALF], 1.0 / O),
                    wait=[t_sum] + list(row_war))
                tv2 = p.bank_read("vector", [7], lambda v: v.tensor_scalar_mul(
                    v2, pb7[0:1, 0:HALF], 1.0 / O), wait=[t_sq])
                return settle(tv2)

            def finish_hstats(ts1, tr_dep):
                tnm = p.emit("vector", lambda v: v.tensor_scalar_mul(
                    nm, m2, -1.0), wait=[ts1], inc="ve")
                tms = p.emit("vector", lambda v: v.scalar_tensor_tensor(
                    w2, m2, 1.0, m2, ALU.mult, ALU.mult), wait=[ts1],
                    inc="ve")
                ts2 = settle(tms)
                t_nm = p.emit("vector", lambda v: v.tensor_sub(v2, v2, w2),
                              wait=[ts2], inc="ve")
                ts = p.emit("scalar", lambda s: s.activation(
                    a2, v2, AF.Sqrt, bias=econ[0:1, 1:2]),
                    wait=[t_nm, t_ones], inc="sc")
                trc = p.emit("vector", lambda v: v.reciprocal(w2, a2),
                             wait=[ts], inc="ve")
                ts3 = settle(trc)
                ta = settle(p.emit("vector", lambda v: v.tensor_scalar_mul(
                    a2, w2, gval[0:1, 4:5]), wait=[ts3, t_ld1], inc="ve"))
                a2c = S("a2c" + str(len(p.ops["tensor"])), NJH)
                tc = None
                for u in range(NJH):
                    p.pe_write([4], lambda t, u=u: t.transpose(
                        pbigB[:, 300:301],
                        a2[0:1, u * 128:(u + 1) * 128], identf[0:1, 0:1]),
                        wait=[ta, t_idf] + list(tr_dep))
                    tc = p.bank_read("vector", [4],
                                     lambda v, u=u: v.tensor_copy(
                                         a2c[:, u:u + 1],
                                         pbigB[:, 300:301]))
                return a2c, settle(tnm), settle(tc)

            def mm2_half(half, a2c, t_nm, t_a2c, gelu_last):
                ev_toks = []
                for ot in range(4):
                    tok_mm = [None] * NJH
                    for oc in range(OC):
                        wb, war = w2_rot.get(ot * OC + oc)
                        tl = p.emit("sync", lambda s, oc=oc, ot=ot, wb=wb:
                                    s.dma_start(out=wb, in_=tpr_in[
                                        oc * 128:(oc + 1) * 128,
                                        ot * 512:(ot + 1) * 512]),
                                    wait=war,
                                    inc=f"w2l{(ot * OC + oc) % 2}", dma=True)
                        for jj in range(NJH):
                            lh = h_fm[:, oc * HALF + jj * 128:
                                      oc * HALF + jj * 128 + 128]
                            if oc == 0:
                                tok_mm[jj] = p.pe_write(
                                    [jj], lambda t, lh=lh, wb=wb, jj=jj:
                                    t.matmul(pb512[jj], lh, wb,
                                             start=True, stop=False),
                                    wait=[tl, gelu_last],
                                    inc="pe" if jj == NJH - 1 else None)
                            else:
                                tok_mm[jj] = p.pe_acc(
                                    [jj], lambda t, lh=lh, wb=wb, jj=jj:
                                    t.matmul(pb512[jj], lh, wb,
                                             start=False, stop=False),
                                    wait=[tl],
                                    inc="pe" if (oc == OC - 1 or
                                                 jj == NJH - 1) else None)
                        w2_rot.used(ot * OC + oc, tok_mm[NJH - 1])
                    for jj in range(NJH):
                        j = half * NJH + jj
                        tr1 = p.pe_acc([jj], lambda t, jj=jj, ot=ot: t.matmul(
                            pb512[jj], nm[0:1, jj * 128:(jj + 1) * 128],
                            s2row[0:1, ot * 512:(ot + 1) * 512],
                            start=False, stop=True),
                            wait=[tok_mm[jj], t_nm, t_ld1], inc="pe")
                        xv = x_tm[:, j * D + ot * 512: j * D + ot * 512 + 512]
                        te = p.bank_read(
                            "vector", [jj],
                            lambda v, jj=jj, xv=xv: v.scalar_tensor_tensor(
                                xv, pb512[jj], a2c[:, jj:jj + 1], xv,
                                ALU.mult, ALU.add),
                            wait=[tr1, t_a2c])
                        ev_toks.append(te)
                return ev_toks

            for half in range(2):
                gelu_toks = []
                for oc in range(OC):
                    tv_, war = tT_rot.get(oc)
                    tl = p.emit("sync", lambda s, oc=oc, tv_=tv_: s.dma_start(
                        out=tv_, in_=tfc_in[oc * 128:(oc + 1) * 128, :]),
                        wait=war, inc=f"tld{oc % 2}", dma=True)
                    bk = 2 + oc % 2
                    tmm = mm1s3(half, oc, tv_, [tl, t_ln3[-1]], bk)
                    tT_rot.used(oc, tmm)
                    tg = p.bank_read("scalar", [bk], lambda s, oc=oc, bk=bk:
                                     s.activation(
                                         h_fm[:, oc * HALF:(oc + 1) * HALF],
                                         pb512[bk], AF.Gelu), wait=[tmm])
                    gelu_toks.append(tg)
                hs = None
                for oc in range(OC):
                    hs = hstats_oc(oc, gelu_toks[-1])
                t_ex = extract_hstats(
                    hs[0], hs[1],
                    row_war=recrow_war[0] if half == 0 else [t_res])
                a2c, t_nm, t_a2c = finish_hstats(t_ex, [])
                evs = mm2_half(half, a2c, t_nm, t_a2c, gelu_toks[-1])
                t_res = evs[-1]

        if "s4" in parts or "s6" in parts:
            u_sb = arA[:, 0:8192]
            v_sb = arA[:, 8192:16384]
            gexp_sb = arA[:, 16384:18432]
            G_sb = arA[:, 18432:20480]
            wr_sb = arA[:, 20480:20992]
            e_sb = arA[0:32, 20992:21504]
            t_hfree = [t_res]

            def moe_weights(dep, rnd):
                t_wr = p.emit("gpsimd", lambda g: g.dma_start(
                    out=wr_sb, in_=wr_in), wait=dep, inc=f"ldBg{rnd}",
                    dma=True)
                t_e = p.emit("sync", lambda s: s.dma_start(
                    out=e_sb, in_=ea_in), wait=dep, inc=f"ldBs{rnd}",
                    dma=True)
                return t_wr, t_e

            t_wr, t_e = moe_weights(t_hfree, 0)
            t_u = p.emit("gpsimd", lambda g: g.dma_start(
                out=u_sb, in_=u_in), wait=t_hfree, inc="ldBg0", dma=True)
            t_vf = p.emit("gpsimd", lambda g: g.dma_start(
                out=v_sb, in_=vf_in), wait=t_hfree, inc="ldBg0", dma=True)
            p.emit("sync", lambda s: s.dma_start(out=attr_sb, in_=at_in),
                   inc="ldBs0", dma=True)
            p.emit("sync", lambda s: s.dma_start(out=wa_sb, in_=wa_in),
                   inc="ldBs0", dma=True)
            t_ldB = ("ldBs0", p.counts["ldBs0"])
            t_y = p.pe_write([6], lambda t: t.matmul(
                pb6[0:1, 0:32], wa_sb, attr_sb, start=True, stop=True),
                wait=[t_ldB])
            t_ng = p.bank_read("vector", [6], lambda v: v.tensor_scalar_mul(
                sg_row, pb6[0:1, 0:32], -1.0))
            t_sg = p.emit("scalar", lambda s: s.activation(
                sg_row, sg_row, AF.Exp), wait=[t_ng], inc="sc")
            t_s1 = p.emit("vector", lambda v: v.tensor_scalar_add(
                sg_row, sg_row, 1.0), wait=[t_sg], inc="ve")
            t_s1s = settle(t_s1)
            t_s2 = p.emit("vector", lambda v: v.reciprocal(aw_row, sg_row),
                          wait=[t_s1s], inc="ve")
            t_ab = p.pe_write([6], lambda t: t.matmul(
                pb6[:, 0:32], onesrow, aw_row, start=True, stop=True),
                wait=[t_s2, t_ones])
            t_awb = p.bank_read("vector", [6], lambda v: v.tensor_copy(
                awb, pb6[:, 0:32]))

        def moe_sublayer(tag, t_res_in, t_wr, t_e):
            t_ln = ln_into_fm(tag, [t_res_in], double=False)
            scale_cols = [S(f"{tag}ga{j}") for j in range(NJ)]
            sc_sb = [S(f"{tag}scb{j}", 32) for j in range(NJ)]
            t_sc = []
            for j in range(NJ):
                bk = 6 + j % 2
                tok = None
                for dc in range(DC):
                    fn = lambda t, dc=dc, j=j, bk=bk: t.matmul(
                        pb512[bk][:, 0:32],
                        xn_fm[:, dc * T + j * 128: dc * T + j * 128 + 128],
                        wr_sb[:, dc * 32:(dc + 1) * 32],
                        start=(dc == 0), stop=(dc == DC - 1))
                    if dc == 0:
                        tok = p.pe_write([bk], fn,
                                         wait=[t_wr, t_ln[-1]], inc=None)
                    else:
                        tok = p.pe_acc([bk], fn,
                                       inc="pe" if dc == DC - 1 else None)
                t_sc.append(p.bank_read("vector", [bk],
                                        lambda v, j=j, bk=bk: v.tensor_copy(
                                            sc_sb[j], pb512[bk][:, 0:32]),
                                        wait=[tok]))
            mtmp = mscf[:, 2048:2816]
            gbf = msc[:, 5632:5888]
            gt_sb = msc[0:32, 5888:6400]
            eb = [mtmp[:, j * 32:(j + 1) * 32] for j in range(NJ)]
            mb = [mtmp[:, 256 + j * 32:256 + (j + 1) * 32] for j in range(NJ)]
            gm_ = [mtmp[:, 512 + j * 32:512 + (j + 1) * 32]
                   for j in range(NJ)]
            cs = {n: [S(f"{tag}{n}{j}") for j in range(NJ)]
                  for n in ("m1", "n1", "sm", "rs", "m2", "r2", "rd", "cg")}
            t0 = [p.emit("vector", lambda v, j=j: v.reduce_max(
                cs["m1"][j], sc_sb[j], axis=AX), wait=[t_sc[j]], inc="ve")
                for j in range(NJ)][-1]
            t1 = [p.emit("vector", lambda v, j=j: v.tensor_scalar_mul(
                cs["n1"][j], cs["m1"][j], -1.0), wait=[t0], inc="ve")
                for j in range(NJ)][-1]
            ts0 = settle(t1)
            te = [p.emit("scalar", lambda s, j=j: s.activation(
                eb[j], sc_sb[j], AF.Exp, bias=cs["n1"][j]),
                wait=[ts0], inc="sc") for j in range(NJ)][-1]
            t2 = [p.emit("vector", lambda v, j=j: v.reduce_sum(
                cs["sm"][j], eb[j], axis=AX), wait=[te], inc="ve")
                for j in range(NJ)][-1]
            t4 = [p.emit("vector", lambda v, j=j: v.tensor_single_scalar(
                mb[j], sc_sb[j], cs["m1"][j], ALU.is_ge),
                wait=[ts0], inc="ve") for j in range(NJ)][-1]
            t5 = [p.emit("vector", lambda v, j=j: v.scalar_tensor_tensor(
                mb[j], mb[j], -1e30, sc_sb[j], ALU.mult, ALU.add),
                wait=[t4], inc="ve") for j in range(NJ)][-1]
            t6 = [p.emit("vector", lambda v, j=j: v.reduce_max(
                cs["m2"][j], mb[j], axis=AX), wait=[t5], inc="ve")
                for j in range(NJ)][-1]
            t3 = [p.emit("vector", lambda v, j=j: v.reciprocal(
                cs["rs"][j], cs["sm"][j]), wait=[t2], inc="ve")
                for j in range(NJ)][-1]
            ts1 = settle(t6)
            ts2 = settle(t3)
            t7 = [p.emit("vector", lambda v, j=j: v.tensor_single_scalar(
                mb[j], sc_sb[j], cs["m2"][j], ALU.is_ge),
                wait=[ts1], inc="ve") for j in range(NJ)][-1]
            t8 = [p.emit("vector", lambda v, j=j: v.tensor_mul(
                gm_[j], eb[j], mb[j]), wait=[t7], inc="ve")
                for j in range(NJ)][-1]
            t9 = [p.emit("vector", lambda v, j=j: v.tensor_single_scalar(
                gm_[j], gm_[j], cs["rs"][j], ALU.mult),
                wait=[t8, ts2], inc="ve") for j in range(NJ)][-1]
            t10 = [p.emit("vector", lambda v, j=j: v.reduce_sum(
                cs["r2"][j], gm_[j], axis=AX), wait=[t9], inc="ve")
                for j in range(NJ)][-1]
            t11 = [p.emit("vector", lambda v, j=j: v.tensor_scalar_add(
                cs["r2"][j], cs["r2"][j], 1e-8), wait=[t10], inc="ve")
                for j in range(NJ)][-1]
            t12 = [p.emit("vector", lambda v, j=j: v.reciprocal(
                cs["rd"][j], cs["r2"][j]), wait=[t11], inc="ve")
                for j in range(NJ)][-1]
            ts3 = settle(t12)
            t13 = [p.emit("vector", lambda v, j=j: v.tensor_single_scalar(
                gm_[j], gm_[j], cs["rd"][j], ALU.mult),
                wait=[ts3], inc="ve") for j in range(NJ)][-1]
            t14 = [p.emit("vector", lambda v, j=j: v.tensor_mul(
                eb[j], gm_[j], awb), wait=[t13, t_awb], inc="ve")
                for j in range(NJ)][-1]
            t15 = [p.emit("vector", lambda v, j=j: v.reduce_sum(
                cs["cg"][j], eb[j], axis=AX), wait=[t14], inc="ve")
                for j in range(NJ)][-1]
            tsc = [p.emit("vector", lambda v, j=j: v.tensor_scalar(
                scale_cols[j], cs["cg"][j], 0.1, 0.45, ALU.mult, ALU.add),
                wait=[t15], inc="ve") for j in range(NJ)][-1]
            g_toks = [p.emit("vector", lambda v, j=j: v.tensor_copy(
                gbf[:, j * 32:(j + 1) * 32], gm_[j]), wait=[t13], inc="ve")
                for j in range(NJ)]
            t_scale = settle(tsc)

            ev = None
            prev_gemm = []
            prev_G = []
            prev_out = []
            for half in range(2):
                tgt = []
                for tj in range(4):
                    j = half * 4 + tj
                    bk = 2 if tj % 2 == 0 else 4
                    view = pb_bf[bk][0:32, (tj % 2) * 128:(tj % 2) * 128 + 128]
                    p.pe_write([bk], lambda t, j=j, view=view: t.transpose(
                        view, gbf[:, j * 32:(j + 1) * 32], ident),
                        wait=[g_toks[j], t_id])
                    tgt.append(p.bank_read(
                        "vector", [bk], lambda v, tj=tj, view=view:
                        v.tensor_copy(gt_sb[0:32, tj * 128:(tj + 1) * 128],
                                      view), wait=prev_gemm))
                t_ge = []
                gemm = []
                for c in range(4):
                    bk = c % 2
                    tok = p.pe_write([bk], lambda t, c=c, bk=bk: t.matmul(
                        pb512[bk], e_sb[:, c * 128:(c + 1) * 128],
                        gt_sb[:, 0:512], start=True, stop=True),
                        wait=tgt + [t_e])
                    gemm.append(tok)
                    t_ge.append(p.bank_read(
                        "scalar", [bk], lambda s, c=c, bk=bk: s.copy(
                            gexp_sb[:, c * 512:(c + 1) * 512], pb512[bk]),
                        wait=[tok] + prev_G))
                prev_gemm = gemm
                t_G = []
                for c in range(4):
                    bk = 2 + c
                    tok = None
                    for dc in range(DC):
                        fn = lambda t, dc=dc, c=c, bk=bk, half=half: t.matmul(
                            pb512[bk],
                            u_sb[:, dc * 512 + c * 128: dc * 512 + c * 128 + 128],
                            xn_fm[:, dc * T + half * 512:
                                  dc * T + half * 512 + 512],
                            start=(dc == 0), stop=(dc == DC - 1))
                        if dc == 0:
                            tok = p.pe_write([bk], fn,
                                             wait=[t_u, t_ln[-1]], inc=None)
                        else:
                            tok = p.pe_acc([bk], fn,
                                           inc="pe" if dc == DC - 1 else None)
                    t_G.append(p.bank_read(
                        "vector", [bk], lambda v, c=c, bk=bk: v.tensor_mul(
                            G_sb[:, c * 512:(c + 1) * 512], pb512[bk],
                            gexp_sb[:, c * 512:(c + 1) * 512]),
                        wait=[tok, t_ge[c]] + prev_out))
                prev_G = t_G
                if "dbgm" in parts and half == 0 and tag == "L4":
                    td1 = p.emit("scalar", lambda s: s.dma_start(
                        out=dbg_moe, in_=arA[:, 16384:20480]),
                        wait=[t_G[3]], inc="st", dma=True)
                    td2 = p.emit("scalar", lambda s: s.dma_start(
                        out=dbg_gt, in_=gt_sb), wait=[t_G[3]],
                        inc="st", dma=True)
                    td3 = p.emit("scalar", lambda s: s.dma_start(
                        out=dbg_gbf, in_=gbf), wait=[g_toks[-1]],
                        inc="st", dma=True)
                    p.emit("scalar", lambda s: s.dma_start(
                        out=dbg_usb, in_=u_sb), wait=[t_G[3]],
                        inc="st", dma=True)
                    p.emit("scalar", lambda s: s.dma_start(
                        out=dbg_xnf, in_=xn_fm), wait=[t_G[3]],
                        inc="st", dma=True)
                    prev_G = t_G + [td1]
                    prev_gemm = prev_gemm + [td2]
                out_toks = []
                for tj in range(4):
                    j = half * 4 + tj
                    for ot in range(4):
                        bk = (0, 1, 6, 7)[(tj * 4 + ot) % 4]
                        tok = None
                        for c in range(4):
                            fn = lambda t, c=c, tj=tj, ot=ot, bk=bk: t.matmul(
                                pb512[bk],
                                G_sb[:, c * 512 + tj * 128:
                                     c * 512 + tj * 128 + 128],
                                v_sb[:, c * D + ot * 512:
                                     c * D + ot * 512 + 512],
                                start=(c == 0), stop=(c == 3))
                            if c == 0:
                                tok = p.pe_write([bk], fn,
                                                 wait=[t_vf, t_G[3]],
                                                 inc=None)
                            else:
                                tok = p.pe_acc([bk], fn,
                                               inc="pe" if c == 3 else None)
                        out_toks.append(tok)
                        xv = x_tm[:, j * D + ot * 512:
                                  j * D + ot * 512 + 512]
                        ev = p.bank_read(
                            "vector", [bk],
                            lambda v, xv=xv, bk=bk, j=j:
                            v.scalar_tensor_tensor(
                                xv, pb512[bk], scale_cols[j], xv,
                                ALU.mult, ALU.add),
                            wait=[tok, t_scale, t_res_in])
                prev_out = [out_toks[-1]]
            return ev

        if "s4" in parts:
            t_res = moe_sublayer("L4", t_res, t_wr, t_e)

        if "s5" in parts:
            t_ln5 = ln_into_fm("L5", [t_res], double=False)
            wv_sb = arA[:, 16384:24576]
            wpp = arA[:, 24576:32768]
            vfm = msc[:, 0:4096]
            arBf = arB.bitcast(F32)
            t_wv = p.emit("gpsimd", lambda g: g.dma_start(
                out=wv_sb, in_=wv_in), wait=[t_res], inc="ldCg", dma=True)
            t_bv = p.emit("sync", lambda s: s.dma_start(out=bvn, in_=bv_in),
                          inc="ldCs", dma=True)
            t_ob = p.emit("sync", lambda s: s.dma_start(
                out=s2row, in_=ob_in), wait=[t_res],
                inc="ldCs", dma=True)
            t_ldC = ("ldCs", p.counts["ldCs"])
            t_bt = p.pe_write([6], lambda t: t.transpose(
                pb6[:, 0:4], bvn, identf[0:4, 0:4]),
                wait=[t_ldC, t_bv, t_idf])
            t_bvc = p.bank_read("vector", [6], lambda v: v.tensor_copy(
                bv_sb, pb6[:, 0:4]))
            t_vfm = []
            vi = 0
            for vc in range(4):
                for hf in range(2):
                    bk = 2 + vi % 4
                    vi += 1
                    tok = None
                    for dc in range(DC):
                        fn = lambda t, dc=dc, vc=vc, hf=hf, bk=bk: t.matmul(
                            pb512[bk],
                            wv_sb[:, dc * 512 + vc * 128:
                                  dc * 512 + vc * 128 + 128],
                            xn_fm[:, dc * T + hf * 512:
                                  dc * T + hf * 512 + 512],
                            start=(dc == 0), stop=(dc == DC - 1))
                        if dc == 0:
                            tok = p.pe_write([bk], fn,
                                             wait=[t_wv, t_ln5[-1]], inc=None)
                        else:
                            tok = p.pe_acc([bk], fn,
                                           inc="pe" if dc == DC - 1 else None)
                    t_vfm.append(p.bank_read(
                        "scalar", [bk], lambda s, vc=vc, hf=hf, bk=bk:
                        s.activation(vfm[:, vc * 1024 + hf * 512:
                                         vc * 1024 + hf * 512 + 512],
                                     pb512[bk], AF.Identity,
                                     bias=bv_sb[:, vc:vc + 1], scale=1.0),
                        wait=[t_bvc]))
            blk_rot = Rot([arBf[:, 0:2048], arBf[:, 2048:4096]])
            blk_rot.seed(t_vfm[-1])
            acc = arBf[:, 4096:6144]
            acc_war = [t_vfm[-1]]
            t_wpp = []
            bi = 0

            def blk_dma(g4, r4):
                nonlocal bi
                bl, bwar = blk_rot.get(bi)
                tok = p.emit(
                    "sync", lambda s, g4=g4, r4=r4, bl=bl: s.dma_start(
                        out=bl, in_=wo1_in[(4 * g4 + r4) * 128:
                                           (4 * g4 + r4) * 128 + 128, :]),
                    wait=bwar, inc=f"wbl{bi % 2}", dma=True)
                bi += 1
                return bl, tok

            for g4 in range(4):
                b0_, tl0 = blk_dma(g4, 0)
                b1_, tl1 = blk_dma(g4, 1)
                ta = p.emit("vector", lambda v, b0_=b0_, b1_=b1_:
                            v.tensor_add(acc, b0_, b1_),
                            wait=[tl0, tl1] + acc_war, inc="ve")
                blk_rot.used(bi - 2, ta)
                blk_rot.used(bi - 1, ta)
                b2_, tl2 = blk_dma(g4, 2)
                ta2 = p.emit("vector", lambda v, b2_=b2_: v.tensor_add(
                    acc, acc, b2_), wait=[ta, tl2], inc="ve")
                blk_rot.used(bi - 1, ta2)
                b3_, tl3 = blk_dma(g4, 3)
                ta3 = p.emit("vector", lambda v, b3_=b3_: v.tensor_add(
                    acc, acc, b3_), wait=[ta2, tl3], inc="ve")
                blk_rot.used(bi - 1, ta3)
                t_wpp.append(p.emit("vector", lambda v, g4=g4: v.tensor_copy(
                    wpp[:, g4 * 2048:(g4 + 1) * 2048], acc),
                    wait=[ta3], inc="ve"))
                acc_war = [t_wpp[-1]]
            ev = None
            for tj in range(NJ):
                for ot in range(4):
                    bk = (tj * 4 + ot) % 2
                    tok = p.pe_write([bk], lambda t, tj=tj, ot=ot, bk=bk:
                                     t.matmul(
                        pb512[bk], vfm[:, tj * 128:tj * 128 + 128],
                        wpp[:, ot * 512:ot * 512 + 512],
                        start=True, stop=False),
                        wait=[t_vfm[-1], t_wpp[0]], inc=None)
                    for vc in range(1, 4):
                        tok = p.pe_acc([bk], lambda t, tj=tj, ot=ot, vc=vc,
                                       bk=bk: t.matmul(
                            pb512[bk],
                            vfm[:, vc * 1024 + tj * 128:
                                vc * 1024 + tj * 128 + 128],
                            wpp[:, vc * 2048 + ot * 512:
                                vc * 2048 + ot * 512 + 512],
                            start=False, stop=False),
                            wait=[t_wpp[vc]], inc=None)
                    tok = p.pe_acc([bk], lambda t, ot=ot, bk=bk: t.matmul(
                        pb512[bk], onesrow[0:1, 0:128],
                        s2row[0:1, ot * 512:(ot + 1) * 512],
                        start=False, stop=True),
                        wait=[t_ob, t_ones], inc="pe")
                    xv = x_tm[:, tj * D + ot * 512: tj * D + ot * 512 + 512]
                    ev = p.bank_read("vector", [bk],
                                     lambda v, xv=xv, bk=bk: v.tensor_add(
                                         xv, pb512[bk], xv), wait=[tok])
            t_res = ev


        if "s6" in parts:
            t_wr6, t_e6 = moe_weights([t_res], 1)
            t_res = moe_sublayer("L6", t_res, t_wr6, t_e6)

        if "dbg" in parts:
            p.emit("scalar", lambda s: s.dma_start(out=dbg_xn, in_=xn_fm),
                   wait=[t_res], inc="st", dma=True)
        p.emit("scalar", lambda s: s.dma_start(
            out=x_out.rearrange("(j p) d -> p j d", p=128),
            in_=x_tm.rearrange("p (j d) -> p j d", d=D)),
            wait=[t_res], inc="st", dma=True)
        n = p.counts.get("st", 0)
        p.emit("sync", lambda s: s.nop(), wait=[("st", n)])

        p.build()
    return nc


def _ident_np():
    return np.eye(128, dtype=ml_dtypes.bfloat16)


def launch1_inmaps(x_full, w_fc, w_proj, w_qkv):
    ident = _ident_np()
    return [{
        "x_in": np.ascontiguousarray(x_full[c * T:(c + 1) * T, :]),
        "w_fc": w_fc, "w_proj": w_proj, "w_qkv": w_qkv,
        "ident": ident,
    } for c in range(NCORES)]


def _tiles(c):
    return [8 * m + c for m in range(NJ)]


def launch2_inmaps(x1_full, q_full, k_full, v_full, w_out0,
                   ternT_fc, ternT_proj, gam, s2row,
                   qkv_w1, qkv_b1, out_w1, out_b1, Wr, U, Vf, attr, Wa):
    bf = ml_dtypes.bfloat16
    ident = _ident_np()
    vg = np.ascontiguousarray(
        v_full.reshape(NKT, 128, NH, 128).transpose(2, 1, 0, 3))
    v_all = np.ascontiguousarray(vg.reshape(NH * 128, NKT * 128))
    mtri = np.where(np.arange(128)[:, None] > np.arange(128)[None, :],
                    -80.0, 0.0).astype(bf)
    wo0 = np.ascontiguousarray(
        w_out0.reshape(NH, 128, 4, 512).transpose(2, 1, 0, 3).reshape(
            4 * 128, NH * 512)).astype(np.float32)
    tfc_t = np.ascontiguousarray(
        ternT_fc.reshape(DC, 128, OC, 128).transpose(2, 1, 0, 3).reshape(
            OC * 128, DC * 128))
    wr_dev = np.ascontiguousarray(
        Wr.T.reshape(DC, 128, 32).transpose(1, 0, 2).reshape(
            128, DC * 32)).astype(np.float32)
    u_flat = U.transpose(1, 0, 2).reshape(D, 512)
    u_dev = np.ascontiguousarray(
        u_flat.reshape(DC, 128, 512).transpose(1, 0, 2).reshape(
            128, DC * 512)).astype(np.float32)
    vf_dev = np.ascontiguousarray(
        Vf.reshape(512, D).reshape(4, 128, D).transpose(1, 0, 2).reshape(
            128, 4 * D)).astype(np.float32)
    attr_t = np.ascontiguousarray(attr.T).astype(np.float32)
    wa_col = np.ascontiguousarray(Wa[0][:, None]).astype(np.float32)
    ea = np.zeros((32, 512), bf)
    for cc in range(4):
        for jj in range(128):
            ea[cc * 8 + jj // 16, cc * 128 + jj] = 1.0
    wv_dev = np.ascontiguousarray(
        qkv_w1[:, 2560:3072].reshape(DC, 128, 512).transpose(1, 0, 2).reshape(
            128, DC * 512)).astype(np.float32)
    bv = np.ascontiguousarray(qkv_b1[2560:3072].reshape(4, 128)).astype(
        np.float32)
    ob = np.ascontiguousarray(out_b1.reshape(1, D)).astype(np.float32)
    k_all = np.ascontiguousarray(k_full.astype(bf))
    tpr = np.ascontiguousarray(ternT_proj.astype(bf))
    wo1 = np.ascontiguousarray(out_w1.astype(np.float32))
    rr = np.repeat(np.arange(8), 128)[None, :]
    maps = []
    for c in range(NCORES):
        tl = _tiles(c)
        x1c = np.ascontiguousarray(
            x1_full.reshape(NKT, 128, D)[tl].reshape(T, D))
        qc = np.ascontiguousarray(
            q_full.reshape(D, NKT, 128)[:, tl].reshape(D, T))
        koc = np.ascontiguousarray(
            k_full.reshape(D, NKT, 128)[:, tl].reshape(D, T))
        voc = np.ascontiguousarray(
            vg[:, :, tl, :].reshape(NH * 128, NJ * 128))
        moff = np.where(rr >= c, -80.0, 0.0).astype(bf)
        maps.append({
            "x1": x1c, "q_fm": qc, "k_all": k_all, "v_all": v_all,
            "k_own": koc, "v_own": voc, "moff": moff, "mtri": mtri,
            "wo0": wo0, "tfc": tfc_t, "tpr": tpr, "gam": gam, "s2in": s2row,
            "wr": wr_dev, "uflat": u_dev, "vflat": vf_dev, "attrT": attr_t,
            "waCol": wa_col, "eAll": ea, "wvT": wv_dev, "bv": bv,
            "wo1": wo1, "outb": ob, "ident": ident,
        })
    return maps


def kernel(**inputs):
    x = np.ascontiguousarray(np.asarray(inputs["L0_attn_x"][0], np.float32))
    w_qkv = np.asarray(inputs["L0_attn_w_qkv"], np.float32)
    w_out = np.asarray(inputs["L0_attn_w_out"], np.float32)
    w_fc = np.asarray(inputs["L0_ffn_w_fc"], np.float32)
    w_proj = np.asarray(inputs["L0_ffn_w_proj"], np.float32)
    qkv_w1 = np.asarray(inputs["L1_attn_qkv_w"], np.float32)
    qkv_b1 = np.asarray(inputs["L1_attn_qkv_b"], np.float32)
    out_w1 = np.asarray(inputs["L1_attn_out_w"], np.float32)
    out_b1 = np.asarray(inputs["L1_attn_out_b"], np.float32)
    Wr = np.asarray(inputs["L1_ffn_Wr"], np.float32)
    U = np.asarray(inputs["L1_ffn_U"], np.float32)
    Vf = np.asarray(inputs["L1_ffn_V"], np.float32)
    attr = np.asarray(inputs["L1_ffn_attr"], np.float32)
    Wa = np.asarray(inputs["L1_ffn_Wa"], np.float32)

    nc1 = build_launch1()
    res1 = run_bass_kernel_spmd(nc1, launch1_inmaps(x, w_fc, w_proj, w_qkv),
                                core_ids=list(range(NCORES)))
    r = res1.results
    x1_full = np.concatenate([np.asarray(r[c]["x1"])
                              for c in range(NCORES)], axis=0)
    q_full = np.concatenate([np.asarray(r[c]["q_fm"])
                             for c in range(NCORES)], axis=1)
    k_full = np.concatenate([np.asarray(r[c]["k_fm"])
                             for c in range(NCORES)], axis=1)
    v_full = np.concatenate([np.asarray(r[c]["v_tm"])
                             for c in range(NCORES)], axis=0)
    gam = np.asarray(r[0]["gam"])
    s2 = np.asarray(r[0]["s2row"])
    tfc = np.asarray(r[0]["ternT_fc"])
    tpr = np.asarray(r[0]["ternT_proj"])

    nc2 = build_launch2()
    maps2 = launch2_inmaps(x1_full, q_full, k_full, v_full, w_out,
                           tfc, tpr, gam, s2, qkv_w1, qkv_b1, out_w1,
                           out_b1, Wr, U, Vf, attr, Wa)
    res2 = run_bass_kernel_spmd(nc2, maps2, core_ids=list(range(NCORES)))
    out = np.empty((SEQ, D), np.float32)
    for c in range(NCORES):
        xo = np.asarray(res2.results[c]["x_out"]).reshape(NJ, 128, D)
        for m in range(NJ):
            t0 = (8 * m + c) * 128
            out[t0:t0 + 128] = xo[m]
    return out[None]
